# version 09
# baseline (speedup 1.0000x reference)
import contextlib
import numpy as np
import ml_dtypes
import concourse.bass as bass
import concourse.mybir as mybir
from concourse.bass_utils import run_bass_kernel_spmd

F32 = mybir.dt.float32
BF16 = mybir.dt.bfloat16
AF = mybir.ActivationFunctionType
ALU = mybir.AluOpType
AX = mybir.AxisListType

T, C, DI, DS, H, E, II = 2048, 1024, 2048, 16, 128, 8, 1024
NKC, NDD, NTT = C // 128, DI // 128, T // 128
EPS = 1e-5
NEGBIG = -30000.0
RS = 1.0 / np.sqrt(128.0)

UPTO = 99
DBG = False

ENGS = ("sync", "scalar", "vector", "tensor", "gpsimd")


class Prog:
    def __init__(self, nc, ctx):
        self.nc = nc
        self.streams = {e: [] for e in ENGS}
        self.sems = {}
        self.cnt = {}
        for name in ("SQ", "AQ", "PE", "VE", "SE", "BR", "PO"):
            self.sems[name] = ctx.enter_context(nc.semaphore(name))
            self.cnt[name] = 0

    def em(self, eng, fn):
        self.streams[eng].append(fn)

    def wait(self, eng, sem, val):
        if val is None or val <= 0:
            return
        s = self.sems[sem]
        self.em(eng, lambda e, s=s, v=val: e.wait_ge(s, v))

    def op(self, eng, fn, sem=None, n=1):
        if sem is None:
            self.em(eng, fn)
            return None
        s = self.sems[sem]
        self.cnt[sem] += n
        v = self.cnt[sem]
        self.em(eng, lambda e, fn=fn, s=s, n=n: fn(e).then_inc(s, n))
        return v

    def fence(self, eng, sem):
        s = self.sems[sem]
        self.cnt[sem] += 1
        v = self.cnt[sem]
        self.em(eng, lambda e, s=s: e.drain().then_inc(s, 1))
        return v

    def drain(self, eng):
        self.em(eng, lambda e: e.drain())

    def barrier(self):
        self.wait("sync", "SQ", self.cnt["SQ"])
        self.wait("scalar", "AQ", self.cnt["AQ"])
        s = self.sems["BR"]
        engs = ("sync", "scalar", "vector", "tensor")
        self.cnt["BR"] += len(engs)
        tgt = self.cnt["BR"]
        for eng in engs:
            self.em(eng, lambda e, s=s: e.drain().then_inc(s, 1))
        for eng in engs:
            self.em(eng, lambda e, s=s, v=tgt: e.wait_ge(s, v))

    def finalize(self):
        nc = self.nc
        streams = self.streams
        with nc.Block() as block:
            @block.sync
            def _(e):
                for fn in streams["sync"]:
                    fn(e)

            @block.scalar
            def _(e):
                for fn in streams["scalar"]:
                    fn(e)

            @block.vector
            def _(e):
                for fn in streams["vector"]:
                    fn(e)

            @block.tensor
            def _(e):
                for fn in streams["tensor"]:
                    fn(e)

            @block.gpsimd
            def _(e):
                for fn in streams["gpsimd"]:
                    fn(e)


def _col(tile, j):
    return tile[:, j:j + 1]


def build(upto=99, dbg=False):
    nc = bass.Bass("TRN2", target_bir_lowering=False)
    ap = {}
    ap["x"] = nc.dram_tensor("x", [T, C], F32, kind="ExternalInput").ap()
    ap["w_in"] = nc.dram_tensor("w_in", [2 * DI, C], F32, kind="ExternalInput").ap()
    ap["conv_w"] = nc.dram_tensor("conv_w", [DI, 4], F32, kind="ExternalInput").ap()
    ap["conv_b"] = nc.dram_tensor("conv_b", [1, DI], F32, kind="ExternalInput").ap()
    ap["a_log"] = nc.dram_tensor("a_log", [DI, DS], F32, kind="ExternalInput").ap()
    ap["w_x"] = nc.dram_tensor("w_x", [2 * DS + 1, DI], F32, kind="ExternalInput").ap()
    ap["w_out"] = nc.dram_tensor("w_out", [C, DI], F32, kind="ExternalInput").ap()
    ap["ln1_w"] = nc.dram_tensor("ln1_w", [1, C], F32, kind="ExternalInput").ap()
    ap["ln1_b"] = nc.dram_tensor("ln1_b", [1, C], F32, kind="ExternalInput").ap()
    ap["ln2_w"] = nc.dram_tensor("ln2_w", [1, C], F32, kind="ExternalInput").ap()
    ap["ln2_b"] = nc.dram_tensor("ln2_b", [1, C], F32, kind="ExternalInput").ap()
    ap["sim1"] = nc.dram_tensor("sim1", [C, E], F32, kind="ExternalInput").ap()
    ap["gates1"] = nc.dram_tensor("gates1", [1, E], F32, kind="ExternalInput").ap()
    ap["qp"] = nc.dram_tensor("qp", [E, C, H], F32, kind="ExternalInput").ap()
    ap["kp"] = nc.dram_tensor("kp", [E, C, H], F32, kind="ExternalInput").ap()
    ap["vp"] = nc.dram_tensor("vp", [E, C, H], F32, kind="ExternalInput").ap()
    ap["op"] = nc.dram_tensor("op", [E, H, C], F32, kind="ExternalInput").ap()
    ap["sim2"] = nc.dram_tensor("sim2", [C, E], F32, kind="ExternalInput").ap()
    ap["gates2"] = nc.dram_tensor("gates2", [1, E], F32, kind="ExternalInput").ap()
    ap["w1"] = nc.dram_tensor("w1", [E, C, II], F32, kind="ExternalInput").ap()
    ap["w2"] = nc.dram_tensor("w2", [E, II, C], F32, kind="ExternalInput").ap()
    ap["ident16"] = nc.dram_tensor("ident16", [128, 128], BF16, kind="ExternalInput").ap()
    ap["ident32"] = nc.dram_tensor("ident32", [128, 128], F32, kind="ExternalInput").ap()
    ap["ones16"] = nc.dram_tensor("ones16", [128, 128], BF16, kind="ExternalInput").ap()
    ap["ones32"] = nc.dram_tensor("ones32", [128, 128], F32, kind="ExternalInput").ap()
    ap["sel8"] = nc.dram_tensor("sel8", [E, E * 128], BF16, kind="ExternalInput").ap()
    ap["dmask"] = nc.dram_tensor("dmask", [128, 128], F32, kind="ExternalInput").ap()
    ap["out"] = nc.dram_tensor("out", [T, C], F32, kind="ExternalOutput").ap()
    rows16 = nc.dram_tensor("rows16_scr", [40, T], BF16, kind="Internal").ap()
    rows32 = nc.dram_tensor("rows32_scr", [4, T], F32, kind="Internal").ap()
    c2scr = nc.dram_tensor("c2_scr", [NTT, 128], F32, kind="Internal").ap()
    ge_scr = nc.dram_tensor("ge_scr", [4, E, 128, 8 * 512], BF16, kind="Internal").ap()
    sz_scr = nc.dram_tensor("sz_scr", [DI, T], BF16, kind="Internal").ap()
    xn32_scr = nc.dram_tensor("xn32_scr", [T, C], F32, kind="Internal").ap()
    w1b_scr = nc.dram_tensor("w1b_scr", [E, C, II], BF16, kind="Internal").ap()
    w2b_scr = nc.dram_tensor("w2b_scr", [E, II, C], BF16, kind="Internal").ap()
    qkv_scr = nc.dram_tensor("qkv_scr", [3, E, 128, NKC * 128], BF16, kind="Internal").ap()

    dbg_out = {}

    def dbg_dump(name, tiles, pr, root):
        if not dbg:
            return
        n = len(tiles)
        p, f = tiles[0].shape[0], tiles[0].shape[1]
        dt = tiles[0].dtype
        o = nc.dram_tensor("dbg_" + name, [n * p, f], dt, kind="ExternalOutput").ap()
        dbg_out[name] = (n, p, f)
        for i, tl in enumerate(tiles):
            pr.op("sync", lambda e, o=o, i=i, tl=tl, p=p: e.dma_start(
                out=o[i * p:(i + 1) * p, :], in_=tl[:]), sem="SQ", n=16)

    root = contextlib.ExitStack()
    with root:
        pr = Prog(nc, root)
        sb = lambda name, shape, dt: root.enter_context(nc.sbuf_tensor(name, shape, dt))

        ident16 = sb("c_ident16", [128, 128], BF16)
        ident32 = sb("c_ident32", [128, 128], F32)
        ones16 = sb("c_ones16", [128, 128], BF16)
        ones32 = sb("c_ones32", [128, 128], F32)
        sel8 = sb("c_sel8", [E, E * 128], BF16)
        dmask = sb("c_dmask", [128, 128], F32)
        ln1w = sb("c_ln1w", [128, NKC], F32)
        ln1b = sb("c_ln1b", [128, NKC], F32)
        ln2w = sb("c_ln2w", [128, NKC], F32)
        ln2b = sb("c_ln2b", [128, NKC], F32)
        cbias = sb("c_cbias", [128, NDD], F32)
        cw = sb("c_cw", [128, 4 * NDD], F32)
        A_sb = sb("c_A", [128, DS * NDD], F32)

        for dst, src in ((ident16, "ident16"), (ident32, "ident32"),
                         (ones16, "ones16"), (ones32, "ones32"),
                         (sel8, "sel8"), (dmask, "dmask")):
            pr.op("sync", lambda e, d=dst, s=ap[src]: e.dma_start(out=d[:], in_=s), sem="SQ", n=16)
        for dst, src in ((ln1w, "ln1_w"), (ln1b, "ln1_b"), (ln2w, "ln2_w"), (ln2b, "ln2_b")):
            pr.op("sync", lambda e, d=dst, s=ap[src]: e.dma_start(
                out=d[:], in_=s.rearrange("one (kc p) -> (one p) kc", p=128),
                allow_slow_non_contiguous=True), sem="SQ", n=16)
        pr.op("sync", lambda e, d=cbias, s=ap["conv_b"]: e.dma_start(
            out=d[:], in_=s.rearrange("one (dd p) -> (one p) dd", p=128),
            allow_slow_non_contiguous=True), sem="SQ", n=16)
        for dd in range(NDD):
            pr.op("sync", lambda e, d=cw, s=ap["conv_w"], dd=dd: e.dma_start(
                out=d[:, dd * 4:(dd + 1) * 4], in_=s[dd * 128:(dd + 1) * 128, :]), sem="SQ", n=16)
            pr.op("sync", lambda e, d=A_sb, s=ap["a_log"], dd=dd: e.dma_start(
                out=d[:, dd * DS:(dd + 1) * DS], in_=s[dd * 128:(dd + 1) * 128, :]), sem="SQ", n=16)
        v_const = pr.cnt["SQ"]
        cst = sb("c_cst", [128, 8], F32)
        for j, val in ((0, float(EPS)), (1, -0.5), (2, -1.0), (3, 1.0)):
            pr.op("vector", lambda e, j=j, val=val: e.memset(cst[:, j:j + 1], val))
        v_cst = pr.op("vector", lambda e: e.memset(cst[:, 4:5], float(RS)), sem="VE")
        pr.wait("scalar", "VE", v_cst)
        for j, val in ((0, float(EPS)), (1, -0.5), (2, -1.0), (3, 1.0), (4, float(RS))):
            nc.const_aps.register(F32, val, cst[:, j:j + 1])
        CEPS = _col(cst, 0)
        CEPSr = cst[0:1, 0:1]
        CNHr = cst[0:1, 1:2]
        CN1r = cst[0:1, 2:3]
        C1r = cst[0:1, 3:4]
        CNH = _col(cst, 1)
        CN1 = _col(cst, 2)
        C1 = _col(cst, 3)
        CRS = _col(cst, 4)
        pr.wait("scalar", "SQ", v_const)
        se_a = pr.op("scalar", lambda e, d=A_sb: e.activation(out=d[:], in_=d[:], func=AF.Exp), sem="SE")
        pr.wait("vector", "SE", se_a)
        ve_a = pr.op("vector", lambda e, d=A_sb: e.tensor_scalar_mul(out=d[:], in0=d[:], scalar1=-1.0), sem="VE")

        xnT = [sb(f"xnT_{k}", [128, T], BF16) for k in range(NKC)]

        p0_ctx = contextlib.ExitStack()
        sb0 = lambda name, shape, dt: p0_ctx.enter_context(nc.sbuf_tensor(name, shape, dt, side="right"))
        xt = [sb0(f"p0_x_{i}", [128, C], F32) for i in range(2)]
        xn32b = [sb0(f"p0_xn32_{i}", [128, C], F32) for i in range(2)]
        xn32_dma = [0, 0]
        xn16 = [sb0(f"p0_xn_{i}", [128, C], BF16) for i in range(2)]
        stat = sb0("p0_stat", [128, 16], F32)
        sqb = sb0("p0_sq", [128, C], F32)
        with nc.psum_tensor("p0_tr0", [128, 128], BF16) as tr_ps0, \
             nc.psum_tensor("p0_tr1", [128, 128], BF16) as tr_ps1:
            tr_ps2 = [tr_ps0, tr_ps1]
            tr_ev = [0, 0]
            last_xn_evict = [0, 0]
            last_pe_for_buf = [0, 0]
            for tt in range(NTT):
                b = tt % 2
                pr.wait("sync", "VE", last_xn_evict[b])
                v_ld = pr.op("sync", lambda e, d=xt[b], s=ap["x"], tt=tt: e.dma_start(
                    out=d[:], in_=s[tt * 128:(tt + 1) * 128, :]), sem="SQ", n=16)
                pr.wait("vector", "SQ", v_ld)
                pr.op("vector", lambda e, x=xt[b]: e.tensor_mul(out=sqb[:], in0=x[:], in1=x[:]))
                pr.op("vector", lambda e, x=xt[b]: e.reduce_sum(out=_col(stat, 0), in_=x[:], axis=AX.X))
                pr.op("vector", lambda e: e.reduce_sum(out=_col(stat, 1), in_=sqb[:], axis=AX.X))
                pr.drain("vector")
                pr.op("vector", lambda e: e.tensor_scalar_mul(out=_col(stat, 2), in0=_col(stat, 0), scalar1=1.0 / C))
                pr.op("vector", lambda e: e.tensor_scalar_mul(out=_col(stat, 3), in0=_col(stat, 1), scalar1=1.0 / C))
                pr.drain("vector")
                pr.op("vector", lambda e: e.tensor_mul(out=_col(stat, 4), in0=_col(stat, 2), in1=_col(stat, 2)))
                pr.drain("vector")
                pr.op("vector", lambda e: e.tensor_sub(out=_col(stat, 5), in0=_col(stat, 3), in1=_col(stat, 4)))
                v_va = pr.fence("vector", "VE")
                pr.wait("scalar", "VE", v_va)
                pr.op("scalar", lambda e: e.activation(out=_col(stat, 6), in_=_col(stat, 5), func=AF.Ln, bias=CEPS))
                pr.drain("scalar")
                pr.op("scalar", lambda e: e.activation(out=_col(stat, 6), in_=_col(stat, 6), func=AF.Exp, scale=CNH))
                s_rstd = pr.fence("scalar", "SE")
                pr.wait("vector", "SE", s_rstd)
                pr.op("vector", lambda e: e.tensor_mul(out=_col(stat, 7), in0=_col(stat, 2), in1=_col(stat, 6)))
                pr.drain("vector")
                pr.op("vector", lambda e: e.tensor_scalar_mul(out=_col(stat, 7), in0=_col(stat, 7), scalar1=-1.0))
                pr.op("vector", lambda e: e.tensor_scalar_add(out=_col(stat, 8), in0=_col(stat, 5), scalar1=float(EPS)))
                pr.drain("vector")
                pr.op("vector", lambda e: e.reciprocal(out=_col(stat, 9), in_=_col(stat, 8)))
                pr.drain("vector")
                pr.op("vector", lambda e: e.tensor_mul(out=_col(stat, 10), in0=_col(stat, 5), in1=_col(stat, 9)))
                v_c2 = pr.fence("vector", "VE")
                pr.wait("scalar", "VE", v_c2)
                pr.wait("vector", "PE", last_pe_for_buf[b])
                s_xn = pr.op("vector", lambda e, x=xt[b], o=xn16[b]: e.tensor_scalar(
                    out=o[:], in0=x[:], scalar1=_col(stat, 6), scalar2=_col(stat, 7),
                    op0=ALU.mult, op1=ALU.add), sem="VE")
                pr.wait("vector", "SQ", xn32_dma[b])
                s_xn32 = pr.op("vector", lambda e, x=xt[b], o=xn32b[b]: e.tensor_scalar(
                    out=o[:], in0=x[:], scalar1=_col(stat, 6), scalar2=_col(stat, 7),
                    op0=ALU.mult, op1=ALU.add), sem="VE")
                pr.wait("sync", "VE", s_xn32)
                xn32_dma[b] = pr.op("sync", lambda e, b=b, tt=tt: e.dma_start(
                    out=xn32_scr[tt * 128:(tt + 1) * 128, :], in_=xn32b[b][:]), sem="SQ", n=16)
                last_xn_evict[b] = s_xn32
                pr.op("scalar", lambda e: e.activation(out=_col(stat, 11), in_=_col(stat, 10), func=AF.Ln, bias=CEPS))
                pr.drain("scalar")
                pr.op("scalar", lambda e: e.activation(out=_col(stat, 11), in_=_col(stat, 11), func=AF.Exp, scale=CNH))
                s_c2 = pr.fence("scalar", "SE")
                pe_last = 0
                pr.wait("tensor", "VE", s_xn)
                for kc in range(NKC):
                    pb = kc % 2
                    pr.wait("tensor", "SE", tr_ev[pb])
                    pe_v = pr.op("tensor", lambda e, b=b, kc=kc, pb=pb: e.transpose(
                        out=tr_ps2[pb][:], in_=xn16[b][:, kc * 128:(kc + 1) * 128], identity=ident16[:]), sem="PE")
                    pr.wait("scalar", "PE", pe_v)
                    tr_ev[pb] = pr.op("scalar", lambda e, kc=kc, tt=tt, pb=pb: e.copy(
                        out=xnT[kc][:, tt * 128:(tt + 1) * 128], in_=tr_ps2[pb][:]), sem="SE")
                    pe_last = pe_v
                last_pe_for_buf[b] = pe_last
                pr.wait("sync", "SE", s_c2)
                v_c2dma = pr.op("sync", lambda e, tt=tt: e.dma_start(
                    out=c2scr[tt:tt + 1, :], in_=_col(stat, 11),
                    allow_slow_non_contiguous=True), sem="SQ", n=16)
                pr.wait("scalar", "SQ", v_c2dma)
            pr.barrier()
        c2row = sb0("c2row", [1, T], F32)
        c2row16 = sb("c2row16", [1, T], BF16)
        c2bc16 = sb("c2bc16", [128, T], BF16)
        v_c2r = pr.op("sync", lambda e: e.dma_start(
            out=c2row[:], in_=c2scr.rearrange("a b -> (a b)").unsqueeze(0)), sem="SQ", n=16)
        pr.wait("scalar", "SQ", v_c2r)
        s_c216 = pr.op("scalar", lambda e: e.copy(out=c2row16[:], in_=c2row[:]), sem="SE")
        pr.wait("sync", "SE", s_c216)
        v_c2w = pr.op("sync", lambda e: e.dma_start(out=rows16[32:33, :], in_=c2row16[:]), sem="SQ", n=16)
        pr.wait("sync", "SQ", v_c2w)
        v_c2b = pr.op("sync", lambda e: e.dma_start(
            out=c2bc16[:], in_=rows16[32:33, :].partition_broadcast(128)), sem="SQ", n=16)

        dbg_dump("xnT", xnT, pr, root)
        dbg_dump("c2row", [c2row], pr, root)
        pr.barrier()
        p0_ctx.close()

        if upto <= 0:
            _final_wait(pr)
            pr.finalize()
            return nc, dbg_out

        xc16 = [sb(f"xc_{d}", [128, T], BF16) for d in range(NDD)]
        p1_rows = contextlib.ExitStack()
        sbr = lambda name, shape, dt: p1_rows.enter_context(nc.sbuf_tensor(name, shape, dt, side="right"))
        drow = sbr("p1_drow", [1, T], F32)
        brow = sbr("p1_brow", [DS, T], F32)
        crow = sbr("p1_crow", [DS, T], F32)
        db16 = sbr("p1_db16", [DS, T], BF16)
        c16r = sbr("p1_c16r", [DS, T], BF16)
        dbc16 = sbr("p1_dbc16", [DS, T], F32)
        trow = sbr("p1_trow", [1, T], F32)
        p1_ctx = contextlib.ExitStack()
        sbp = lambda name, shape, dt: p1_ctx.enter_context(nc.sbuf_tensor(name, shape, dt, side="right"))
        wxT = [sbp(f"wxT_{d}", [128, 33], BF16) for d in range(NDD)]
        wnat32 = [sbp(f"p1_wn32_{i}", [128, C], F32) for i in range(2)]
        wnat16 = [sbp(f"p1_wn16_{i}", [128, C], BF16) for i in range(2)]
        wT16 = [[sbp(f"p1_wT{i}_{k}", [128, 128], BF16) for k in range(NKC)] for i in range(2)]
        xp16 = sbp("p1_xp", [128, T], BF16)
        wx32 = sbp("p1_wx32", [33, DI], F32)
        wx16 = sbp("p1_wx16", [33, DI], BF16)
        dgbuf = [sbp(f"p1_dg_{k}", [128, 128], BF16) for k in range(4)]
        zstg = [sbp(f"p1_zstg_{i}", [128, 512], BF16) for i in range(2)]
        zstg_dma = [0, 0]

        with contextlib.ExitStack() as psctx:
            ps = lambda name, shape, dt: psctx.enter_context(nc.psum_tensor(name, shape, dt))
            wtr_ps2 = [ps("p1_wtr0", [128, 128], BF16), ps("p1_wtr1", [128, 128], BF16)]
            wtr_ev = [0, 0]
            xz_ps = [ps(f"p1_xz_{i}", [128, 512], F32) for i in range(2)]
            cv_ps = [ps(f"p1_cv_{i}", [128, 512], F32) for i in range(2)]
            wxp_ps = ps("p1_wxp", [128, 33], BF16)
            v_wx = pr.op("sync", lambda e: e.dma_start(out=wx32[:], in_=ap["w_x"]), sem="SQ", n=16)
            pr.wait("scalar", "SQ", v_wx)
            s_wx = pr.op("scalar", lambda e: e.copy(out=wx16[:], in_=wx32[:]), sem="SE")
            pr.wait("tensor", "SE", s_wx)
            for dd in range(NDD):
                pe_v = pr.op("tensor", lambda e, dd=dd: e.transpose(
                    out=wxp_ps[:], in_=wx16[0:33, dd * 128:(dd + 1) * 128],
                    identity=ident16[0:33, 0:33]), sem="PE")
                pr.wait("scalar", "PE", pe_v)
                s_ev = pr.op("scalar", lambda e, dd=dd: e.copy(out=wxT[dd][:], in_=wxp_ps[:]), sem="SE")
                pr.wait("tensor", "SE", s_ev)
            cast_done = [0, 0]
            wt_done = [0, 0]
            mm_done = [0, 0]
            xz_ev = [0, 0]
            xz_mm = [0, 0]
            cv_ev = [0, 0]
            cv_mm = [0, 0]
            conv_pe_done = 0
            for me in range(2 * NDD):
                buf = me % 2
                pr.wait("sync", "SE", cast_done[buf])
                v_ld = pr.op("sync", lambda e, d=wnat32[buf], me=me: e.dma_start(
                    out=d[:], in_=ap["w_in"][me * 128:(me + 1) * 128, :]), sem="SQ", n=16)
                pr.wait("scalar", "SQ", v_ld)
                pr.wait("scalar", "PE", mm_done[buf])
                cast_done[buf] = pr.op("scalar", lambda e, b=buf: e.copy(
                    out=wnat16[b][:], in_=wnat32[b][:]), sem="SE")
                pr.wait("tensor", "SE", cast_done[buf])
                pr.wait("tensor", "PE", mm_done[buf])
                for kc in range(NKC):
                    pb = kc % 2
                    pr.wait("tensor", "SE", wtr_ev[pb])
                    pe_v = pr.op("tensor", lambda e, b=buf, kc=kc, pb=pb: e.transpose(
                        out=wtr_ps2[pb][:], in_=wnat16[b][:, kc * 128:(kc + 1) * 128], identity=ident16[:]), sem="PE")
                    pr.wait("scalar", "PE", pe_v)
                    wtr_ev[pb] = pr.op("scalar", lambda e, b=buf, kc=kc, pb=pb: e.copy(
                        out=wT16[b][kc][:], in_=wtr_ps2[pb][:]), sem="SE")
                wt_done[buf] = pr.cnt["SE"]
                pr.wait("tensor", "SE", wt_done[buf])
                last_mm = 0
                for tc in range(4):
                    pb = tc % 2
                    pr.wait("tensor", "SE", xz_ev[pb])
                    for kc in range(NKC):
                        last_mm = pr.op("tensor", lambda e, b=buf, kc=kc, tc=tc, pb=pb: e.matmul(
                            out=xz_ps[pb][:], lhsT=wT16[b][kc][:],
                            rhs=xnT[kc][:, tc * 512:(tc + 1) * 512],
                            start=(kc == 0), stop=(kc == NKC - 1)), sem="PE" if kc == NKC - 1 else None)
                    xz_mm[pb] = last_mm
                    pr.wait("scalar", "PE", xz_mm[pb])
                    if me < NDD:
                        xz_ev[pb] = pr.op("scalar", lambda e, me=me, tc=tc, pb=pb: e.copy(
                            out=xp16[:, tc * 512:(tc + 1) * 512], in_=xz_ps[pb][:]), sem="SE")
                    else:
                        zz = me - NDD
                        pr.wait("scalar", "SQ", zstg_dma[pb])
                        xz_ev[pb] = pr.op("scalar", lambda e, pb=pb: e.activation(
                            out=zstg[pb][:], in_=xz_ps[pb][:], func=AF.Silu), sem="SE")
                        pr.wait("sync", "SE", xz_ev[pb])
                        zstg_dma[pb] = pr.op("sync", lambda e, zz=zz, tc=tc, pb=pb: e.dma_start(
                            out=sz_scr[zz * 128:(zz + 1) * 128, tc * 512:(tc + 1) * 512],
                            in_=zstg[pb][:]), sem="SQ", n=16)
                mm_done[buf] = last_mm
                if me < NDD:
                    dd = me
                    pr.wait("scalar", "PE", conv_pe_done)
                    dg_done = 0
                    for k in range(4):
                        dg_done = pr.op("scalar", lambda e, dd=dd, k=k: e.mul(
                            out=dgbuf[k][:], in_=ident16[:], mul=_col(cw, dd * 4 + k)), sem="SE")
                    pr.wait("tensor", "SE", dg_done)
                    conv_last = 0
                    for tc in range(4):
                        pb = tc % 2
                        pr.wait("tensor", "SE", cv_ev[pb])
                        lo = tc * 512
                        for k in (3, 2, 1, 0):
                            sh = 3 - k
                            if tc == 0 and sh > 0:
                                conv_last = pr.op("tensor", lambda e, k=k, sh=sh, pb=pb: e.matmul(
                                    out=cv_ps[pb][:, sh:512], lhsT=dgbuf[k][:],
                                    rhs=xp16[:, 0:512 - sh],
                                    start=False, stop=(k == 0)), sem="PE" if k == 0 else None)
                            else:
                                conv_last = pr.op("tensor", lambda e, k=k, sh=sh, lo=lo, pb=pb: e.matmul(
                                    out=cv_ps[pb][:], lhsT=dgbuf[k][:],
                                    rhs=xp16[:, lo - sh:lo + 512 - sh],
                                    start=(k == 3), stop=(k == 0)), sem="PE" if k == 0 else None)
                        cv_mm[pb] = conv_last
                        pr.wait("scalar", "PE", cv_mm[pb])
                        cv_ev[pb] = pr.op("scalar", lambda e, dd=dd, tc=tc, pb=pb: e.activation(
                            out=xc16[dd][:, tc * 512:(tc + 1) * 512], in_=cv_ps[pb][:],
                            func=AF.Silu, bias=_col(cbias, dd)), sem="SE")
                    conv_pe_done = conv_last
                    pr.wait("scalar", "PE", conv_pe_done)
            pr.barrier()
        with contextlib.ExitStack() as psctx2:
            with contextlib.ExitStack() as ps2:
                d_ps = ps2.enter_context(nc.psum_tensor("p1_dps", [1, 512], F32))
                b_ps = ps2.enter_context(nc.psum_tensor("p1_bps", [DS, 512], F32))
                c_ps = ps2.enter_context(nc.psum_tensor("p1_cps", [DS, 512], F32))
                for tc in range(4):
                    last = 0
                    for dd in range(NDD):
                        st8 = (dd == 0)
                        sp8 = (dd == NDD - 1)
                        pr.op("tensor", lambda e, dd=dd, tc=tc, st8=st8, sp8=sp8: e.matmul(
                            out=d_ps[:], lhsT=wxT[dd][:, 0:1],
                            rhs=xc16[dd][:, tc * 512:(tc + 1) * 512], start=st8, stop=sp8))
                        pr.op("tensor", lambda e, dd=dd, tc=tc, st8=st8, sp8=sp8: e.matmul(
                            out=b_ps[:], lhsT=wxT[dd][:, 1:17],
                            rhs=xc16[dd][:, tc * 512:(tc + 1) * 512], start=st8, stop=sp8))
                        last = pr.op("tensor", lambda e, dd=dd, tc=tc, st8=st8, sp8=sp8: e.matmul(
                            out=c_ps[:], lhsT=wxT[dd][:, 17:33],
                            rhs=xc16[dd][:, tc * 512:(tc + 1) * 512], start=st8, stop=sp8), sem="PE")
                    pr.wait("scalar", "PE", last)
                    pr.op("scalar", lambda e, tc=tc: e.copy(out=drow[0:1, tc * 512:(tc + 1) * 512], in_=d_ps[:]))
                    pr.op("scalar", lambda e, tc=tc: e.copy(out=brow[:, tc * 512:(tc + 1) * 512], in_=b_ps[:]))
                    s_cr = pr.op("scalar", lambda e, tc=tc: e.copy(out=crow[:, tc * 512:(tc + 1) * 512], in_=c_ps[:]), sem="SE")
                    pr.wait("tensor", "SE", s_cr)
            pr.barrier()
        p1_ctx.close()
        pr.op("scalar", lambda e: e.activation(out=trow[:], in_=drow[:], func=AF.Abs))
        pr.drain("scalar")
        pr.op("scalar", lambda e: e.activation(out=trow[:], in_=trow[:], func=AF.Exp, scale=CN1r))
        pr.drain("scalar")
        s_t = pr.op("scalar", lambda e: e.activation(out=trow[:], in_=trow[:], func=AF.Ln, bias=C1r), sem="SE")
        pr.wait("vector", "SE", s_t)
        pr.op("vector", lambda e: e.tensor_scalar_max(out=drow[:], in0=drow[:], scalar1=0.0))
        pr.drain("vector")
        v_d = pr.op("vector", lambda e: e.tensor_add(out=drow[:], in0=drow[:], in1=trow[:]), sem="VE")
        pr.wait("sync", "VE", v_d)
        v_dw = pr.op("sync", lambda e: e.dma_start(out=rows32[0:1, :], in_=drow[:]), sem="SQ", n=16)
        pr.wait("sync", "SQ", v_dw)
        v_db = pr.op("sync", lambda e: e.dma_start(
            out=dbc16[:], in_=rows32[0:1, :].partition_broadcast(DS)), sem="SQ", n=16)
        pr.wait("vector", "SQ", v_db)
        v_dbm = pr.op("vector", lambda e: e.tensor_mul(out=db16[:], in0=brow[:], in1=dbc16[:]), sem="VE")
        s_c16 = pr.op("scalar", lambda e: e.copy(out=c16r[:], in_=crow[:]), sem="SE")
        pr.wait("sync", "VE", v_dbm)
        pr.op("sync", lambda e: e.dma_start(out=rows16[0:DS, :], in_=db16[:]), sem="SQ", n=16)
        pr.wait("sync", "SE", s_c16)
        v_rows = pr.op("sync", lambda e: e.dma_start(out=rows16[DS:2 * DS, :], in_=c16r[:]), sem="SQ", n=16)

        dbg_dump("xc", xc16, pr, root)
        if dbg:
            o_sz = nc.dram_tensor("dbg_sz", [DI, T], BF16, kind="ExternalOutput").ap()
            pr.op("sync", lambda e: e.dma_start(out=o_sz, in_=sz_scr), sem="SQ", n=16)
        dbg_dump("drow", [drow], pr, root)
        dbg_dump("dBrow", [db16], pr, root)
        dbg_dump("Crow", [c16r], pr, root)
        pr.barrier()
        p1_rows.close()

        if upto <= 1:
            _final_wait(pr)
            pr.finalize()
            return nc, dbg_out

        p2_ctx = contextlib.ExitStack()
        sb2 = lambda name, shape, dt: p2_ctx.enter_context(nc.sbuf_tensor(name, shape, dt, side="right"))
        delta_bc = sb2("p2_deltabc", [128, T], F32)
        dAb = [sb2(f"p2_dA_{i}", [128, T], F32) for i in range(2)]
        dBub = [sb2(f"p2_dBu_{i}", [128, T], BF16) for i in range(2)]
        hb = [sb2(f"p2_h_{i}", [128, T], BF16) for i in range(2)]
        Chb = [sb2(f"p2_Ch_{i}", [128, T], BF16) for i in range(2)]
        dbbc = [sb2(f"p2_dbbc_{i}", [128, T], BF16) for i in range(2)]
        cbc = [sb2(f"p2_cbc_{i}", [128, T], BF16) for i in range(2)]
        szt = [sb2(f"p2_szt_{i}", [128, T], BF16) for i in range(2)]
        pc32 = [sb2(f"p2_pc32_{i}", [128, C], F32) for i in range(2)]
        pc16 = [sb2(f"p2_pc16_{i}", [128, C], BF16) for i in range(2)]
        cast_steps = []
        for ee in range(E):
            for kc in range(NKC):
                cast_steps.append((ap["w1"][ee][kc * 128:(kc + 1) * 128, :],
                                   w1b_scr[ee][kc * 128:(kc + 1) * 128, :], None))
        for ee in range(E):
            for ki in range(NKC):
                cast_steps.append((ap["w2"][ee][ki * 128:(ki + 1) * 128, :],
                                   w2b_scr[ee][ki * 128:(ki + 1) * 128, :], None))
        for pi, wname in enumerate(("qp", "kp", "vp")):
            for ee in range(E):
                cast_steps.append((ap[wname][ee].rearrange("(kc p) h -> p kc h", p=128),
                                   qkv_scr[pi, ee], "r"))
        pc_state = {"i": 0, "cast": [0, 0], "st": [0, 0]}

        def emit_cast_step():
            i = pc_state["i"]
            if i >= len(cast_steps):
                return
            pc_state["i"] = i + 1
            srcap, dstap, mode = cast_steps[i]
            b = i % 2
            pr.wait("sync", "SE", pc_state["cast"][b])
            if mode == "r":
                v = pr.op("sync", lambda e, s=srcap, b=b: e.dma_start(
                    out=pc32[b][:].rearrange("p (kc h) -> p kc h", h=128), in_=s), sem="SQ", n=16)
            else:
                v = pr.op("sync", lambda e, s=srcap, b=b: e.dma_start(out=pc32[b][:], in_=s), sem="SQ", n=16)
            pr.wait("scalar", "SQ", max(v, pc_state["st"][b]))
            pc_state["cast"][b] = pr.op("scalar", lambda e, b=b: e.copy(out=pc16[b][:], in_=pc32[b][:]), sem="SE")
            pr.wait("sync", "SE", pc_state["cast"][b])
            pc_state["st"][b] = pr.op("sync", lambda e, d=dstap, b=b: e.dma_start(out=d, in_=pc16[b][:]), sem="SQ", n=16)
        v_dbc = pr.op("sync", lambda e: e.dma_start(
            out=delta_bc[:], in_=rows32[0:1, :].partition_broadcast(128)), sem="SQ", n=16)
        with contextlib.ExitStack() as psc:
            y_ps = [psc.enter_context(nc.psum_tensor(f"p2_y_{i}", [128, 512], F32)) for i in range(8)]
            dA_free = [0, 0]
            ch_free = [0, 0]
            bc_used = [0, 0]
            bc_used_po = [0, 0]
            po_dbu = [0, 0]
            bc_ld = [0, 0]
            sz_free = 0
            sz_ld = 0
            yp_done = 0
            it = 0
            for ddp in range(NDD // 2):
                dd0 = 2 * ddp
                pr.wait("sync", "VE", sz_free)
                for j in range(2):
                    sz_ld = pr.op("sync", lambda e, j=j, dd0=dd0: e.dma_start(
                        out=szt[j][:], in_=sz_scr[(dd0 + j) * 128:(dd0 + j + 1) * 128, :]), sem="SQ", n=16)
                def issue_bcast(s):
                    jb = s % 2
                    pr.wait("sync", "VE", bc_used[jb])
                    pr.wait("sync", "PO", bc_used_po[jb])
                    pr.op("sync", lambda e, s=s, jb=jb: e.dma_start(
                        out=dbbc[jb][:], in_=rows16[s:s + 1, :].partition_broadcast(128)), sem="SQ", n=16)
                    bc_ld[jb] = pr.op("sync", lambda e, s=s, jb=jb: e.dma_start(
                        out=cbc[jb][:], in_=rows16[DS + s:DS + s + 1, :].partition_broadcast(128)), sem="SQ", n=16)
                issue_bcast(0)
                for s in range(DS):
                    jb = s % 2
                    emit_cast_step()
                    if s + 1 < DS:
                        issue_bcast(s + 1)
                    for q in range(2):
                        dd = dd0 + q
                        ib = it % 2
                        pr.wait("scalar", "VE", dA_free[ib])
                        pr.wait("scalar", "SQ", v_dbc)
                        s_dA = pr.op("scalar", lambda e, ib=ib, dd=dd, s=s: e.activation(
                            out=dAb[ib][:], in_=delta_bc[:], func=AF.Exp,
                            scale=A_sb[:, dd * DS + s:dd * DS + s + 1]), sem="SE")
                        pr.wait("gpsimd", "SQ", bc_ld[jb])
                        pr.wait("gpsimd", "VE", dA_free[ib])
                        po_dbu[ib] = pr.op("gpsimd", lambda e, ib=ib, dd=dd, jb=jb: e.tensor_mul(
                            out=dBub[ib][:], in0=xc16[dd][:], in1=dbbc[jb][:]), sem="PO")
                        if q == 1:
                            bc_used_po[jb] = po_dbu[ib]
                        pr.wait("vector", "SE", s_dA)
                        pr.wait("vector", "PO", po_dbu[ib])
                        dA_free[ib] = pr.op("vector", lambda e, ib=ib: e.tensor_tensor_scan(
                            out=hb[ib][:], data0=dAb[ib][:], data1=dBub[ib][:], initial=0.0,
                            op0=ALU.mult, op1=ALU.add), sem="VE")
                        pr.wait("vector", "PE", ch_free[ib])
                        v_ch = pr.op("vector", lambda e, ib=ib, jb=jb: e.tensor_mul(
                            out=Chb[ib][:], in0=hb[ib][:], in1=cbc[jb][:]), sem="VE")
                        if q == 1:
                            bc_used[jb] = v_ch
                        pr.wait("tensor", "VE", v_ch)
                        if s == 0:
                            pr.wait("tensor", "VE", yp_done)
                        last = 0
                        for tc in range(4):
                            last = pr.op("tensor", lambda e, q=q, tc=tc, ib=ib, s=s: e.matmul(
                                out=y_ps[q * 4 + tc][:], lhsT=ident16[:],
                                rhs=Chb[ib][:, tc * 512:(tc + 1) * 512],
                                start=(s == 0), stop=(s == DS - 1)),
                                sem="PE" if tc == 3 else None)
                        ch_free[ib] = last
                        it += 1
                pr.wait("vector", "PE", ch_free[0])
                pr.wait("vector", "PE", ch_free[1])
                pr.wait("vector", "SQ", sz_ld)
                for q in range(2):
                    for tc in range(4):
                        yp_done = pr.op("vector", lambda e, q=q, tc=tc, dd0=dd0: e.tensor_mul(
                            out=xc16[dd0 + q][:, tc * 512:(tc + 1) * 512],
                            in0=y_ps[q * 4 + tc][:], in1=szt[q][:, tc * 512:(tc + 1) * 512]), sem="VE")
                sz_free = yp_done
            while pc_state["i"] < len(cast_steps):
                emit_cast_step()
            pr.barrier()
        p2_ctx.close()

        dbg_dump("yp", xc16, pr, root)
        if upto <= 2:
            _final_wait(pr)
            pr.finalize()
            return nc, dbg_out

        sout16 = [sb(f"sout_{m}", [128, T], BF16) for m in range(NKC)]
        p3_ctx = contextlib.ExitStack()
        sb3 = lambda name, shape, dt: p3_ctx.enter_context(nc.sbuf_tensor(name, shape, dt, side="right"))
        woT = [sb3(f"p3_woT_{d}", [128, C], BF16) for d in range(NDD)]
        wo32 = [sb3(f"p3_wo32_{i}", [128, DI], F32) for i in range(2)]
        wo16 = [sb3(f"p3_wo16_{i}", [128, DI], BF16) for i in range(2)]
        with contextlib.ExitStack() as psc:
            wtr_ps2 = [psc.enter_context(nc.psum_tensor(f"p3_wtr{i}", [128, 128], BF16)) for i in range(2)]
            wtr_ev = [0, 0]
            so_ps = [psc.enter_context(nc.psum_tensor(f"p3_so_{i}", [128, 512], F32)) for i in range(2)]
            cast_done = [0, 0]
            for mt in range(NKC):
                buf = mt % 2
                pr.wait("sync", "SE", cast_done[buf])
                v_ld = pr.op("sync", lambda e, b=buf, mt=mt: e.dma_start(
                    out=wo32[b][:], in_=ap["w_out"][mt * 128:(mt + 1) * 128, :]), sem="SQ", n=16)
                pr.wait("scalar", "SQ", v_ld)
                cast_done[buf] = pr.op("scalar", lambda e, b=buf: e.copy(
                    out=wo16[b][:], in_=wo32[b][:]), sem="SE")
                pr.wait("tensor", "SE", cast_done[buf])
                for dd in range(NDD):
                    pb = dd % 2
                    pr.wait("tensor", "SE", wtr_ev[pb])
                    pe_v = pr.op("tensor", lambda e, b=buf, dd=dd, pb=pb: e.transpose(
                        out=wtr_ps2[pb][:], in_=wo16[b][:, dd * 128:(dd + 1) * 128], identity=ident16[:]), sem="PE")
                    pr.wait("scalar", "PE", pe_v)
                    wtr_ev[pb] = pr.op("scalar", lambda e, dd=dd, mt=mt, pb=pb: e.copy(
                        out=woT[dd][:, mt * 128:(mt + 1) * 128], in_=wtr_ps2[pb][:]), sem="SE")
            pr.barrier()
            so_ev = [0, 0]
            for tc in range(4):
                for mt in range(NKC):
                    pb = mt % 2
                    pr.wait("tensor", "SE", so_ev[pb])
                    last = 0
                    for dd in range(NDD):
                        last = pr.op("tensor", lambda e, dd=dd, mt=mt, tc=tc, pb=pb: e.matmul(
                            out=so_ps[pb][:], lhsT=woT[dd][:, mt * 128:(mt + 1) * 128],
                            rhs=xc16[dd][:, tc * 512:(tc + 1) * 512],
                            start=(dd == 0), stop=(dd == NDD - 1)),
                            sem="PE" if dd == NDD - 1 else None)
                    pr.wait("scalar", "PE", last)
                    so_ev[pb] = pr.op("scalar", lambda e, mt=mt, tc=tc, pb=pb: e.copy(
                        out=sout16[mt][:, tc * 512:(tc + 1) * 512], in_=so_ps[pb][:]), sem="SE")
            pr.barrier()
        p3_ctx.close()

        dbg_dump("sout", sout16, pr, root)
        if upto <= 3:
            _final_wait(pr)
            pr.finalize()
            return nc, dbg_out

        hs16 = xc16[0:NKC]
        n1T = xc16[NKC:2 * NKC]

        for kc in range(NKC):
            pr.wait("vector", "SQ", v_c2b)
            pr.op("vector", lambda e, kc=kc: e.tensor_mul(
                out=n1T[kc][:], in0=xnT[kc][:], in1=c2bc16[:]))
            pr.drain("vector")
            pr.op("vector", lambda e, kc=kc: e.tensor_scalar(
                out=n1T[kc][:], in0=n1T[kc][:], scalar1=_col(ln1w, kc), scalar2=_col(ln1b, kc),
                op0=ALU.mult, op1=ALU.add))
        v_n1 = pr.fence("vector", "VE")

        def gating(sim_ap, gates_ap, nsrc, rw16out, sbg, tag, v_ready, fp32=False, affine=None):
            sim32 = sbg(tag + "_sim32", [128, E * NKC], F32)
            nsim16 = sbg(tag + "_nsim16", [128, E * NKC], BF16)
            if fp32:
                xst32 = sbg(tag + "_xst32", [128, C], F32)
                xnT32c = [sbg(tag + f"_xnT32_{k}", [128, 512], F32) for k in range(NKC)]
                c2b32 = sbg(tag + "_c2b32", [128, 512], F32)
                nsq32 = sbg(tag + "_nsq32", [128, 512], F32)
                c2flat = c2scr.rearrange("a b -> (a b)").unsqueeze(0)
                aw, ab = affine
            rnb8 = sbg(tag + "_rnb8", [128, E], F32)
            sgr = sbg(tag + "_sgr", [1, E], F32)
            nrm = sbg(tag + "_nrm", [1, E], F32)
            colsq = sbg(tag + "_colsq", [1, E * NKC], F32)
            nsq16 = sbg(tag + "_nsq", [128, T], BF16)
            stub8 = sbg(tag + "_stub8", [8, 512], F32)
            stub1 = sbg(tag + "_stub1", [1, 512], F32)
            G = sbg(tag + "_G", [128, 128], F32)
            Gs = [sbg(tag + f"_Gs{i}", [128, 128], F32) for i in range(3)]
            Gm = [sbg(tag + f"_Gm{i}", [128, 16], F32) for i in range(2)]
            rnG = sbg(tag + "_rnG", [128, 16], F32)
            sgbig = sbg(tag + "_sgbig", [128, 128], F32)
            gz = sbg(tag + "_gz", [128, 128], F32)
            g3 = lambda t: t[:].rearrange("p (a b) -> p a b", b=E)
            gm3 = lambda t: t[:].unsqueeze(2).broadcast_to([128, 16, E])
            pr.op("vector", lambda e: e.memset(gz[:], 0.0))
            v_ld = 0
            for kc in range(NKC):
                v_ld = pr.op("sync", lambda e, kc=kc: e.dma_start(
                    out=sim32[:, kc * E:(kc + 1) * E], in_=sim_ap[kc * 128:(kc + 1) * 128, :]), sem="SQ", n=16)
            v_g = pr.op("sync", lambda e: e.dma_start(out=sgr[:], in_=gates_ap), sem="SQ", n=16)
            pr.wait("scalar", "SQ", v_g)
            pr.op("scalar", lambda e: e.activation(out=sgr[:], in_=sgr[:], func=AF.Sigmoid))
            s_sg = pr.fence("scalar", "SE")
            pr.wait("sync", "SE", s_sg)
            v_w = pr.op("sync", lambda e: e.dma_start(out=rows32[1:2, 0:E], in_=sgr[:]), sem="SQ", n=16)
            pr.wait("sync", "SQ", v_w)
            for j in range(16):
                pr.op("sync", lambda e, j=j: e.dma_start(
                    out=sgbig[:, j * E:(j + 1) * E],
                    in_=rows32[1:2, 0:E].partition_broadcast(128)), sem="SQ", n=16)
            v_sgb = pr.cnt["SQ"]
            with contextlib.ExitStack() as psg:
                cs_ps = psg.enter_context(nc.psum_tensor(tag + "_csps", [1, E * NKC], F32))
                nq_ps = [psg.enter_context(nc.psum_tensor(tag + f"_nq{i}", [1, 512], F32)) for i in range(1 if fp32 else 4)]
                raw_ps = psg.enter_context(nc.psum_tensor(tag + "_rawps", [8, 512], F32))
                gt_ps = psg.enter_context(nc.psum_tensor(tag + "_gtps", [128, 8], F32))
                gb_ps = psg.enter_context(nc.psum_tensor(tag + "_gbps", [8, 128], F32))
                pr.wait("vector", "SQ", v_ld)
                pr.op("vector", lambda e: e.tensor_mul(out=Gs[0][:, 0:E * NKC], in0=sim32[:], in1=sim32[:]))
                v_sq = pr.fence("vector", "VE")
                pr.wait("tensor", "VE", v_sq)
                pe_cs = pr.op("tensor", lambda e: e.matmul(
                    out=cs_ps[:], lhsT=ones32[:, 0:1], rhs=Gs[0][:, 0:E * NKC],
                    start=True, stop=True), sem="PE")
                pr.wait("scalar", "PE", pe_cs)
                s_cs = pr.op("scalar", lambda e: e.copy(out=colsq[:], in_=cs_ps[:]), sem="SE")
                pr.wait("vector", "SE", s_cs)
                pr.op("vector", lambda e: e.tensor_reduce(
                    out=nrm[:].unsqueeze(2),
                    in_=colsq[:].rearrange("p (kc e) -> p e kc", e=E), axis=AX.X, op=ALU.add))
                v_nr = pr.fence("vector", "VE")
                pr.wait("scalar", "VE", v_nr)
                pr.op("scalar", lambda e: e.activation(out=nrm[:], in_=nrm[:], func=AF.Ln))
                pr.drain("scalar")
                pr.op("scalar", lambda e: e.activation(out=nrm[:], in_=nrm[:], func=AF.Exp, scale=CNHr))
                s_nr = pr.fence("scalar", "SE")
                pr.wait("sync", "SE", s_nr)
                v_nw = pr.op("sync", lambda e: e.dma_start(out=rows32[2:3, 0:E], in_=nrm[:]), sem="SQ", n=16)
                pr.wait("sync", "SQ", v_nw)
                v_nb = pr.op("sync", lambda e: e.dma_start(
                    out=rnb8[:], in_=rows32[2:3, 0:E].partition_broadcast(128)), sem="SQ", n=16)
                pr.wait("vector", "SQ", v_nb)
                for kc in range(NKC):
                    pr.op("vector", lambda e, kc=kc: e.tensor_mul(
                        out=nsim16[:, kc * E:(kc + 1) * E], in0=sim32[:, kc * E:(kc + 1) * E], in1=rnb8[:]))
                if fp32:
                    pr.drain("vector")
                    for kc in range(NKC):
                        pr.op("vector", lambda e, kc=kc: e.tensor_mul(
                            out=sim32[:, kc * E:(kc + 1) * E], in0=sim32[:, kc * E:(kc + 1) * E], in1=rnb8[:]))
                v_ns = pr.fence("vector", "VE")
                if fp32:
                    tr32_ps2 = [psg.enter_context(nc.psum_tensor(tag + f"_tr32{i}", [128, 128], F32)) for i in range(2)]
                    tr32_ev = [0, 0]
                    pr.wait("tensor", "VE", v_ns)
                    v_rw_prev = 0
                    ev_prev = 0
                    v_aff_prev = 0
                    for tc in range(4):
                        x_dma = 0
                        for ttl in range(4):
                            tt = tc * 4 + ttl
                            pr.wait("sync", "PE", ev_prev)
                            x_dma = pr.op("sync", lambda e, tt=tt: e.dma_start(
                                out=xst32[:], in_=xn32_scr[tt * 128:(tt + 1) * 128, :]), sem="SQ", n=16)
                            pr.wait("tensor", "SQ", x_dma)
                            for kc in range(NKC):
                                pb = kc % 2
                                pr.wait("tensor", "SE", tr32_ev[pb])
                                pe_t = pr.op("tensor", lambda e, kc=kc, pb=pb: e.transpose(
                                    out=tr32_ps2[pb][:], in_=xst32[:, kc * 128:(kc + 1) * 128],
                                    identity=ident32[:]), sem="PE")
                                pr.wait("scalar", "PE", pe_t)
                                tr32_ev[pb] = pr.op("scalar", lambda e, kc=kc, ttl=ttl, pb=pb: e.copy(
                                    out=xnT32c[kc][:, ttl * 128:(ttl + 1) * 128], in_=tr32_ps2[pb][:]), sem="SE")
                            ev_prev = pr.cnt["PE"]
                        ev_all = pr.cnt["SE"]
                        pr.wait("sync", "VE", v_aff_prev)
                        v_c2c = pr.op("sync", lambda e, tc=tc: e.dma_start(
                            out=c2b32[:], in_=c2flat[0:1, tc * 512:(tc + 1) * 512].partition_broadcast(128)), sem="SQ", n=16)
                        pr.wait("vector", "SQ", v_c2c)
                        pr.wait("vector", "SE", ev_all)
                        for kc in range(NKC):
                            pr.op("vector", lambda e, kc=kc: e.tensor_mul(
                                out=xnT32c[kc][:], in0=xnT32c[kc][:], in1=c2b32[:]))
                            pr.drain("vector")
                            pr.op("vector", lambda e, kc=kc: e.tensor_scalar(
                                out=xnT32c[kc][:], in0=xnT32c[kc][:], scalar1=_col(aw, kc), scalar2=_col(ab, kc),
                                op0=ALU.mult, op1=ALU.add))
                        v_aff = pr.fence("vector", "VE")
                        v_aff_prev = v_aff
                        pr.wait("scalar", "VE", v_aff)
                        nq_last = 0
                        for kc in range(NKC):
                            s_sq = pr.op("scalar", lambda e, kc=kc: e.activation(
                                out=nsq32[:], in_=xnT32c[kc][:], func=AF.Square), sem="SE")
                            pr.wait("tensor", "SE", s_sq)
                            nq_last = pr.op("tensor", lambda e, kc=kc: e.matmul(
                                out=nq_ps[0][:], lhsT=ones32[:, 0:1], rhs=nsq32[:],
                                start=(kc == 0), stop=(kc == NKC - 1)), sem="PE")
                            pr.wait("scalar", "PE", nq_last)
                        pr.wait("scalar", "SQ", v_rw_prev)
                        pr.op("scalar", lambda e: e.copy(out=stub1[:], in_=nq_ps[0][:]))
                        pr.drain("scalar")
                        pr.op("scalar", lambda e: e.activation(out=stub1[:], in_=stub1[:], func=AF.Ln))
                        pr.drain("scalar")
                        pr.op("scalar", lambda e: e.activation(out=stub1[:], in_=stub1[:], func=AF.Exp, scale=CNHr))
                        s_rn = pr.fence("scalar", "SE")
                        pr.wait("sync", "SE", s_rn)
                        v_rw_prev = pr.op("sync", lambda e, tc=tc: e.dma_start(
                            out=rows32[3:4, tc * 512:(tc + 1) * 512], in_=stub1[:]), sem="SQ", n=16)
                        pr.wait("tensor", "SE", s_rn)
                        raw_last = 0
                        for kc in range(NKC):
                            raw_last = pr.op("tensor", lambda e, kc=kc, tc=tc: e.matmul(
                                out=raw_ps[:], lhsT=sim32[:, kc * E:(kc + 1) * E],
                                rhs=xnT32c[kc][:], start=(kc == 0), stop=(kc == NKC - 1)),
                                sem="PE" if kc == NKC - 1 else None)
                        pr.wait("scalar", "PE", raw_last)
                        s_st = pr.op("scalar", lambda e: e.copy(out=stub8[:], in_=raw_ps[:]), sem="SE")
                        pr.wait("tensor", "SE", s_st)
                        for j in range(4):
                            pe_g = pr.op("tensor", lambda e, j=j: e.transpose(
                                out=gt_ps[:], in_=stub8[:, j * 128:(j + 1) * 128],
                                identity=ident32[0:8, 0:8]), sem="PE")
                            pr.wait("scalar", "PE", pe_g)
                            s_gv = pr.op("scalar", lambda e, tc=tc, j=j: e.copy(
                                out=G[:, (tc * 4 + j) * E:(tc * 4 + j + 1) * E], in_=gt_ps[:]), sem="SE")
                            pr.wait("tensor", "SE", s_gv)
                        ev_prev = pr.cnt["PE"]
                        pr.barrier()
                    pr.wait("sync", "SQ", pr.cnt["SQ"])
                    v_rng = pr.op("sync", lambda e: e.dma_start(
                        out=rnG[:], in_=rows32[3:4, :].rearrange("one (tt p) -> (one p) tt", p=128),
                        allow_slow_non_contiguous=True), sem="SQ", n=16)
                if not fp32:
                    pr.wait("scalar", "VE", v_ready)
                    nq_last = 0
                    for kc in range(NKC):
                        s_sq = pr.op("scalar", lambda e, kc=kc: e.activation(
                            out=nsq16[:], in_=nsrc[kc][:], func=AF.Square), sem="SE")
                        pr.wait("tensor", "SE", s_sq)
                        for tc in range(4):
                            nq_last = pr.op("tensor", lambda e, kc=kc, tc=tc: e.matmul(
                                out=nq_ps[tc][:], lhsT=ones16[:, 0:1],
                                rhs=nsq16[:, tc * 512:(tc + 1) * 512],
                                start=(kc == 0), stop=(kc == NKC - 1)),
                                sem="PE" if (kc == NKC - 1) else None)
                        if kc < NKC - 1:
                            nq_mark = pr.op("tensor", lambda e: e.nop(), sem="PE")
                            pr.wait("scalar", "PE", nq_mark)
                    v_rw_prev = 0
                    for tc in range(4):
                        pr.wait("scalar", "PE", nq_last)
                        pr.wait("scalar", "SQ", v_rw_prev)
                        pr.op("scalar", lambda e, tc=tc: e.copy(out=stub1[:], in_=nq_ps[tc][:]))
                        pr.drain("scalar")
                        pr.op("scalar", lambda e: e.activation(out=stub1[:], in_=stub1[:], func=AF.Ln))
                        pr.drain("scalar")
                        pr.op("scalar", lambda e: e.activation(out=stub1[:], in_=stub1[:], func=AF.Exp, scale=CNHr))
                        s_rn = pr.fence("scalar", "SE")
                        pr.wait("sync", "SE", s_rn)
                        v_rw_prev = pr.op("sync", lambda e, tc=tc: e.dma_start(
                            out=rows32[3:4, tc * 512:(tc + 1) * 512], in_=stub1[:]), sem="SQ", n=16)
                    pr.barrier()
                    pr.wait("sync", "SQ", pr.cnt["SQ"])
                    v_rng = pr.op("sync", lambda e: e.dma_start(
                        out=rnG[:], in_=rows32[3:4, :].rearrange("one (tt p) -> (one p) tt", p=128),
                        allow_slow_non_contiguous=True), sem="SQ", n=16)
                    pr.wait("tensor", "VE", v_ns)
                    for tc in range(4):
                        raw_last = 0
                        for kc in range(NKC):
                            raw_last = pr.op("tensor", lambda e, kc=kc, tc=tc: e.matmul(
                                out=raw_ps[:], lhsT=nsim16[:, kc * E:(kc + 1) * E],
                                rhs=nsrc[kc][:, tc * 512:(tc + 1) * 512],
                                start=(kc == 0), stop=(kc == NKC - 1)),
                                sem="PE" if kc == NKC - 1 else None)
                        pr.wait("scalar", "PE", raw_last)
                        s_st = pr.op("scalar", lambda e: e.copy(out=stub8[:], in_=raw_ps[:]), sem="SE")
                        pr.wait("tensor", "SE", s_st)
                        for j in range(4):
                            pe_g = pr.op("tensor", lambda e, j=j: e.transpose(
                                out=gt_ps[:], in_=stub8[:, j * 128:(j + 1) * 128],
                                identity=ident32[0:8, 0:8]), sem="PE")
                            pr.wait("scalar", "PE", pe_g)
                            s_gv = pr.op("scalar", lambda e, tc=tc, j=j: e.copy(
                                out=G[:, (tc * 4 + j) * E:(tc * 4 + j + 1) * E], in_=gt_ps[:]), sem="SE")
                            pr.wait("tensor", "SE", s_gv)

                pr.barrier()
                v_sgready = v_sgb
                pr.wait("vector", "SE", pr.cnt["SE"])
                pr.wait("vector", "SQ", max(v_rng, v_sgready))
                D = lambda: pr.drain("vector")
                pr.op("vector", lambda e: e.tensor_mul(out=g3(G), in0=g3(G), in1=gm3(rnG))); D()
                pr.op("vector", lambda e: e.tensor_sub(out=G[:], in0=G[:], in1=sgbig[:])); D()
                if dbg:
                    o_g = nc.dram_tensor("dbg_logits" + tag, [128, 128], F32, kind="ExternalOutput").ap()
                    v_gd = pr.fence("vector", "VE")
                    pr.wait("sync", "VE", v_gd)
                    pr.op("sync", lambda e, o_g=o_g: e.dma_start(out=o_g, in_=G[:]), sem="SQ", n=16)
                    o_rn = nc.dram_tensor("dbg_rng" + tag, [128, 16], F32, kind="ExternalOutput").ap()
                    pr.op("sync", lambda e, o_rn=o_rn: e.dma_start(out=o_rn, in_=rnG[:]), sem="SQ", n=16)
                    o_sg = nc.dram_tensor("dbg_sgb" + tag, [128, 128], F32, kind="ExternalOutput").ap()
                    pr.op("sync", lambda e, o_sg=o_sg: e.dma_start(out=o_sg, in_=sgbig[:]), sem="SQ", n=16)
                pr.op("vector", lambda e: e.tensor_reduce(out=Gm[0][:].unsqueeze(2), in_=g3(G), axis=AX.X, op=ALU.max)); D()
                pr.op("vector", lambda e: e.tensor_tensor(out=g3(Gs[0]), in0=g3(G), in1=gm3(Gm[0]), op=ALU.is_equal)); D()
                pr.op("vector", lambda e: e.scalar_tensor_tensor(out=Gs[1][:], in0=Gs[0][:], scalar=NEGBIG, in1=G[:], op0=ALU.mult, op1=ALU.add)); D()
                pr.op("vector", lambda e: e.tensor_reduce(out=Gm[1][:].unsqueeze(2), in_=g3(Gs[1]), axis=AX.X, op=ALU.max)); D()
                pr.op("vector", lambda e: e.tensor_tensor(out=g3(Gs[1]), in0=g3(Gs[1]), in1=gm3(Gm[1]), op=ALU.is_equal)); D()
                pr.op("vector", lambda e: e.tensor_add(out=Gs[0][:], in0=Gs[0][:], in1=Gs[1][:])); D()
                pr.op("vector", lambda e: e.tensor_tensor(out=Gs[1][:], in0=G[:], in1=gz[:], op=ALU.is_gt)); D()
                pr.op("vector", lambda e: e.tensor_reduce(out=Gm[0][:].unsqueeze(2), in_=g3(Gs[1]), axis=AX.X, op=ALU.add)); D()
                pr.op("vector", lambda e: e.tensor_tensor(out=Gm[1][:], in0=Gm[0][:], in1=gz[:, 0:16], op=ALU.is_equal)); D()
                pr.op("vector", lambda e: e.tensor_tensor(out=g3(Gs[0]), in0=g3(Gs[0]), in1=gm3(Gm[1]), op=ALU.mult)); D()
                pr.op("vector", lambda e: e.tensor_add(out=Gs[1][:], in0=Gs[1][:], in1=Gs[0][:])); D()
                pr.op("vector", lambda e: e.tensor_scalar_max(out=Gs[2][:], in0=G[:], scalar1=0.0)); D()
                pr.op("vector", lambda e: e.tensor_mul(out=Gs[2][:], in0=Gs[2][:], in1=Gs[1][:])); D()
                pr.op("vector", lambda e: e.scalar_tensor_tensor(out=Gs[2][:], in0=Gs[1][:], scalar=-NEGBIG, in1=Gs[2][:], op0=ALU.mult, op1=ALU.add)); D()
                pr.op("vector", lambda e: e.tensor_scalar_add(out=Gs[2][:], in0=Gs[2][:], scalar1=NEGBIG))
                v_mk = pr.fence("vector", "VE")
                pr.wait("scalar", "VE", v_mk)
                pr.op("scalar", lambda e: e.activation(out=Gs[2][:], in_=Gs[2][:], func=AF.Exp))
                s_ex = pr.fence("scalar", "SE")
                pr.wait("vector", "SE", s_ex)
                pr.op("vector", lambda e: e.tensor_reduce(out=Gm[0][:].unsqueeze(2), in_=g3(Gs[2]), axis=AX.X, op=ALU.add)); D()
                pr.op("vector", lambda e: e.reciprocal(out=Gm[1][:], in_=Gm[0][:])); D()
                pr.op("vector", lambda e: e.tensor_mul(out=g3(G), in0=g3(Gs[2]), in1=gm3(Gm[1])))
                v_rwg = pr.fence("vector", "VE")
                pr.wait("tensor", "VE", v_rwg)
                for j in range(16):
                    pe_b = pr.op("tensor", lambda e, j=j: e.transpose(
                        out=gb_ps[:], in_=G[:, j * E:(j + 1) * E], identity=ident32[:]), sem="PE")
                    pr.wait("scalar", "PE", pe_b)
                    s_rv = pr.op("scalar", lambda e, j=j: e.copy(
                        out=rw16out[0:8, j * 128:(j + 1) * 128], in_=gb_ps[:]), sem="SE")
                    pr.wait("tensor", "SE", s_rv)
                return pr.fence("scalar", "SE")

        p4_ctx = contextlib.ExitStack()
        sb4 = lambda nm, shape, dt: p4_ctx.enter_context(nc.sbuf_tensor(nm, shape, dt, side="right"))
        rwA = sb4("p4_rwA", [8, T], BF16)
        g1_scope = contextlib.ExitStack()
        sbg1 = lambda nm, shape, dt: g1_scope.enter_context(nc.sbuf_tensor(nm, shape, dt, side="right"))
        s_rw = gating(ap["sim1"], ap["gates1"], n1T, rwA, sbg1, "g1", v_n1,
                      fp32=True, affine=(ln1w, ln1b))
        pr.barrier()
        g1_scope.close()

        q16 = sb4("p4_q16", [128, T], BF16)
        k16 = sb4("p4_k16", [128, T], BF16)
        v16 = sb4("p4_v16", [128, T], BF16)
        vT16 = [sb4(f"p4_vT_{i}", [128, 128], BF16) for i in range(NTT)]
        probs16 = sb4("p4_probs", [128, T], BF16)
        n1e = [sb4(f"p4_n1e_{i}", [128, 512], BF16) for i in range(2)]
        rwbc = sb4("p4_rwbc", [128, 512], BF16)
        rwbcq = [sb4(f"p4_rwbcq_{i}", [128, 128], BF16) for i in range(E)]
        ae16 = [sb4(f"p4_ae_{i}", [128, 128], BF16) for i in range(E)]
        exT = [sb4(f"p4_exT_{i}", [128, 128], BF16) for i in range(2)]
        wst32 = sb4("p4_wst32", [128, C], F32)
        wst16 = [sb4(f"p4_wq_{i}", [128, C], BF16) for i in range(3)]
        ow16 = [sb4(f"p4_ow_{i}", [128, C], BF16) for i in range(E)]
        mxt = sb4("p4_mx", [128, 8], F32)
        with contextlib.ExitStack() as ps4:
            qkv_ps = [ps4.enter_context(nc.psum_tensor(f"p4_qkv_{i}", [128, 512], F32)) for i in range(3)]
            rb_ps = ps4.enter_context(nc.psum_tensor("p4_rb", [128, 512], F32))
            v_ow = 0
            s_owc = 0
            for ee in range(E):
                pr.wait("sync", "SE", s_owc)
                v_ow = pr.op("sync", lambda e, ee=ee: e.dma_start(
                    out=wst32[:], in_=ap["op"][ee]), sem="SQ", n=16)
                pr.wait("scalar", "SQ", v_ow)
                s_owc = pr.op("scalar", lambda e, ee=ee: e.copy(out=ow16[ee][:], in_=wst32[:]), sem="SE")
            qkv_ev = 0
            pe_prev_e = 0
            pe_n1e = [0, 0]
            rb_used = 0
            for tc in range(4):
                pr.wait("tensor", "SE", s_rw)
                pr.wait("tensor", "SE", qkv_ev)
                pe_done = 0
                for ee in range(E):
                    pr.wait("tensor", "VE", rb_used)
                    pe_rb = pr.op("tensor", lambda e, ee=ee, tc=tc: e.matmul(
                        out=rb_ps[:], lhsT=sel8[:, ee * 128:(ee + 1) * 128],
                        rhs=rwA[0:8, tc * 512:(tc + 1) * 512], start=True, stop=True), sem="PE")
                    pr.wait("scalar", "PE", pe_rb)
                    s_rb = pr.op("scalar", lambda e: e.copy(out=rwbc[:], in_=rb_ps[:]), sem="SE")
                    s_w = 0
                    pr.wait("sync", "PE", pe_prev_e)
                    for pi in range(3):
                        s_w = pr.op("sync", lambda e, ee=ee, pi=pi: e.dma_start(
                            out=wst16[pi][:], in_=qkv_scr[pi, ee]), sem="SQ", n=16)
                    last3 = 0
                    for kc in range(NKC):
                        ib = kc % 2
                        pr.wait("vector", "SE", s_rb)
                        pr.wait("vector", "PE", pe_n1e[ib])
                        v_ne = pr.op("vector", lambda e, kc=kc, tc=tc, ib=ib: e.tensor_mul(
                            out=n1e[ib][:], in0=n1T[kc][:, tc * 512:(tc + 1) * 512], in1=rwbc[:]), sem="VE")
                        pr.wait("tensor", "VE", v_ne)
                        pr.wait("tensor", "SQ", s_w)
                        for pi in range(3):
                            last3 = pr.op("tensor", lambda e, pi=pi, kc=kc, ee=ee, ib=ib: e.matmul(
                                out=qkv_ps[pi][:], lhsT=wst16[pi][:, kc * 128:(kc + 1) * 128],
                                rhs=n1e[ib][:],
                                start=(ee == 0 and kc == 0), stop=(ee == E - 1 and kc == NKC - 1)),
                                sem="PE" if pi == 2 else None)
                        pe_n1e[ib] = last3
                    pe_prev_e = last3
                    rb_used = pr.cnt["VE"]
                pe_done = pe_prev_e
                pr.wait("scalar", "PE", pe_done)
                pr.op("scalar", lambda e, tc=tc: e.activation(
                    out=q16[:, tc * 512:(tc + 1) * 512], in_=qkv_ps[0][:], func=AF.Copy, scale=float(RS)))
                pr.op("scalar", lambda e, tc=tc: e.copy(out=k16[:, tc * 512:(tc + 1) * 512], in_=qkv_ps[1][:]))
                qkv_ev = pr.op("scalar", lambda e, tc=tc: e.copy(
                    out=v16[:, tc * 512:(tc + 1) * 512], in_=qkv_ps[2][:]), sem="SE")
            pr.barrier()
        with contextlib.ExitStack() as ps5:
            sc_ps = [ps5.enter_context(nc.psum_tensor(f"p4_sc_{i}", [128, 512], F32)) for i in range(4)]
            str_ps = ps5.enter_context(nc.psum_tensor("p4_str", [128, 128], BF16))
            av_ps = ps5.enter_context(nc.psum_tensor("p4_av", [128, 128], F32))
            hs_ps = ps5.enter_context(nc.psum_tensor("p4_hs", [128, 1024], F32))
            for kb in range(NTT):
                pe_v = pr.op("tensor", lambda e, kb=kb: e.transpose(
                    out=str_ps[:], in_=v16[:, kb * 128:(kb + 1) * 128], identity=ident16[:]), sem="PE")
                pr.wait("scalar", "PE", pe_v)
                s_ev = pr.op("scalar", lambda e, kb=kb: e.copy(out=vT16[kb][:], in_=str_ps[:]), sem="SE")
                pr.wait("tensor", "SE", s_ev)
            hs_ev = 0
            for qb in range(NTT):
                W = (qb + 1) * 128
                nch = (W + 511) // 512
                if qb % 4 == 0:
                    pass
                sc_last = []
                for ci in range(nch):
                    c0 = ci * 512
                    c1 = min(W, c0 + 512)
                    v_s = pr.op("tensor", lambda e, qb=qb, ci=ci, c0=c0, c1=c1: e.matmul(
                        out=sc_ps[ci][:, 0:c1 - c0], lhsT=q16[:, qb * 128:(qb + 1) * 128],
                        rhs=k16[:, c0:c1], start=True, stop=True), sem="PE")
                    sc_last.append(v_s)
                off = qb * 128 - (nch - 1) * 512
                pr.wait("vector", "PE", sc_last[-1])
                pr.op("vector", lambda e, off=off, ci=nch - 1: e.tensor_add(
                    out=sc_ps[ci][:, off:off + 128], in0=sc_ps[ci][:, off:off + 128], in1=dmask[:]))
                pr.drain("vector")
                for ci in range(nch):
                    c0 = ci * 512
                    c1 = min(W, c0 + 512)
                    pr.op("vector", lambda e, ci=ci, c0=c0, c1=c1: e.reduce_max(
                        out=_col(mxt, ci), in_=sc_ps[ci][:, 0:c1 - c0], axis=AX.X))
                pr.drain("vector")
                for ci in range(1, nch):
                    pr.op("vector", lambda e, ci=ci: e.tensor_max(out=_col(mxt, 0), in0=_col(mxt, 0), in1=_col(mxt, ci)))
                    pr.drain("vector")
                pr.op("vector", lambda e: e.tensor_scalar_mul(out=_col(mxt, 4), in0=_col(mxt, 0), scalar1=-1.0))
                v_mx = pr.fence("vector", "VE")
                pr.wait("scalar", "VE", v_mx)
                for ci in range(nch):
                    c0 = ci * 512
                    c1 = min(W, c0 + 512)
                    pr.op("scalar", lambda e, ci=ci, c0=c0, c1=c1: e.activation(
                        out=probs16[:, c0:c1], in_=sc_ps[ci][:, 0:c1 - c0], func=AF.Exp,
                        bias=_col(mxt, 4), accum_out=_col(mxt, 8 + ci) if False else _col(mxt, ci)))
                s_ex = pr.fence("scalar", "SE")
                pr.wait("vector", "SE", s_ex)
                for ci in range(1, nch):
                    pr.op("vector", lambda e, ci=ci: e.tensor_add(out=_col(mxt, 0), in0=_col(mxt, 0), in1=_col(mxt, ci)))
                    pr.drain("vector")
                pr.op("vector", lambda e: e.reciprocal(out=_col(mxt, 5), in_=_col(mxt, 0)))
                pr.drain("vector")
                v_pr = pr.op("vector", lambda e, W=W: e.tensor_scalar_mul(
                    out=probs16[:, 0:W], in0=probs16[:, 0:W], scalar1=_col(mxt, 5)), sem="VE")
                pr.wait("tensor", "VE", v_pr)
                rbq_ev = [0, 0]
                for ee in range(E):
                    pbq = ee % 2
                    pr.wait("tensor", "SE", rbq_ev[pbq])
                    pe_rb = pr.op("tensor", lambda e, ee=ee, qb=qb, pbq=pbq: e.matmul(
                        out=sc_ps[pbq][:, 0:128], lhsT=sel8[:, ee * 128:(ee + 1) * 128],
                        rhs=rwA[0:8, qb * 128:(qb + 1) * 128], start=True, stop=True), sem="PE")
                    pr.wait("scalar", "PE", pe_rb)
                    rbq_ev[pbq] = pr.op("scalar", lambda e, ee=ee, pbq=pbq: e.copy(
                        out=rwbcq[ee][:], in_=sc_ps[pbq][:, 0:128]), sem="SE")
                pr.wait("tensor", "SE", rbq_ev[0])
                pr.wait("tensor", "SE", rbq_ev[1])
                av_last = 0
                for kb in range(qb + 1):
                    pe_t = pr.op("tensor", lambda e, kb=kb: e.transpose(
                        out=str_ps[:], in_=probs16[:, kb * 128:(kb + 1) * 128], identity=ident16[:]), sem="PE")
                    pr.wait("scalar", "PE", pe_t)
                    s_x = pr.op("scalar", lambda e, kb=kb: e.copy(out=exT[kb % 2][:], in_=str_ps[:]), sem="SE")
                    pr.wait("tensor", "SE", s_x)
                    av_last = pr.op("tensor", lambda e, kb=kb, qb=qb: e.matmul(
                        out=av_ps[:], lhsT=vT16[kb][:], rhs=exT[kb % 2][:],
                        start=(kb == 0), stop=(kb == qb)), sem="PE")
                pr.wait("vector", "PE", av_last)
                v_ae = 0
                for ee in range(E):
                    v_ae = pr.op("vector", lambda e, ee=ee: e.tensor_mul(
                        out=ae16[ee][:], in0=av_ps[:], in1=rwbcq[ee][:]), sem="VE")
                pr.wait("tensor", "VE", v_ae)
                pr.wait("tensor", "SE", hs_ev)
                for mc in range(NKC):
                    pr.op("tensor", lambda e, mc=mc, qb=qb: e.matmul(
                        out=hs_ps[:, mc * 128:(mc + 1) * 128], lhsT=ident16[:],
                        rhs=xnT[mc][:, qb * 128:(qb + 1) * 128],
                        start=(mc == 0 or mc == 4), stop=False))
                hs_last = 0
                for ee in range(E):
                    for mc in range(NKC):
                        hs_last = pr.op("tensor", lambda e, ee=ee, mc=mc: e.matmul(
                            out=hs_ps[:, mc * 128:(mc + 1) * 128], lhsT=ow16[ee][:, mc * 128:(mc + 1) * 128],
                            rhs=ae16[ee][:], start=False,
                            stop=(ee == E - 1 and mc in (3, 7))),
                            sem="PE" if (ee == E - 1 and mc == NKC - 1) else None)
                pr.wait("scalar", "PE", hs_last)
                for mc in range(NKC):
                    hs_ev = pr.op("scalar", lambda e, mc=mc, qb=qb: e.copy(
                        out=hs16[mc][:, qb * 128:(qb + 1) * 128],
                        in_=hs_ps[:, mc * 128:(mc + 1) * 128]), sem="SE")
            pr.barrier()
        p4_ctx.close()

        dbg_dump("hs", hs16, pr, root)
        dbg_dump("rwrows", [rwA], pr, root)
        dbg_dump("q16", [q16], pr, root)
        dbg_dump("k16", [k16], pr, root)
        dbg_dump("v16", [v16], pr, root)
        dbg_dump("n1T0", [n1T[0]], pr, root)
        if upto <= 4:
            _final_wait(pr)
            pr.finalize()
            return nc, dbg_out

        n2T = xc16[NKC:2 * NKC]
        rwB = root.enter_context(nc.sbuf_tensor("p5_rwB", [8, T], BF16, side="right"))
        p5_ctx = contextlib.ExitStack()
        sb5 = lambda nm, shape, dt: p5_ctx.enter_context(nc.sbuf_tensor(nm, shape, dt, side="right"))
        sq16b = sb5("p5_sq16", [128, T], BF16)
        stubA = sb5("p5_stubA", [1, 512], F32)
        stubB = sb5("p5_stubB", [1, 512], F32)
        stub16 = sb5("p5_stub16", [1, 512], BF16)
        mu2bc = sb5("p5_mu2bc", [128, T], BF16)
        rs2bc = sb5("p5_rs2bc", [128, T], BF16)
        with contextlib.ExitStack() as ps6:
            s1_ps = [ps6.enter_context(nc.psum_tensor(f"p5_s1_{i}", [1, 512], F32)) for i in range(4)]
            s2_ps = [ps6.enter_context(nc.psum_tensor(f"p5_s2_{i}", [1, 512], F32)) for i in range(4)]
            ln2_last = 0
            for kc in range(NKC):
                s_sq = pr.op("scalar", lambda e, kc=kc: e.activation(
                    out=sq16b[:], in_=hs16[kc][:], func=AF.Square), sem="SE")
                pr.wait("tensor", "SE", s_sq)
                for tc in range(4):
                    pr.op("tensor", lambda e, kc=kc, tc=tc: e.matmul(
                        out=s1_ps[tc][:], lhsT=ones16[:, 0:1],
                        rhs=hs16[kc][:, tc * 512:(tc + 1) * 512],
                        start=(kc == 0), stop=(kc == NKC - 1)))
                    ln2_last = pr.op("tensor", lambda e, kc=kc, tc=tc: e.matmul(
                        out=s2_ps[tc][:], lhsT=ones16[:, 0:1],
                        rhs=sq16b[:, tc * 512:(tc + 1) * 512],
                        start=(kc == 0), stop=(kc == NKC - 1)),
                        sem="PE" if kc == NKC - 1 else None)
                if kc < NKC - 1:
                    mk = pr.op("tensor", lambda e: e.nop(), sem="PE")
                    pr.wait("scalar", "PE", mk)
            v_dma_prev = 0
            for tc in range(4):
                pr.wait("scalar", "PE", ln2_last)
                pr.wait("scalar", "SQ", v_dma_prev)
                pr.op("scalar", lambda e, tc=tc: e.mul(out=stubA[:], in_=s1_ps[tc][:], mul=1.0 / C))
                pr.op("scalar", lambda e, tc=tc: e.mul(out=stubB[:], in_=s2_ps[tc][:], mul=1.0 / C))
                s_ab = pr.fence("scalar", "SE")
                pr.wait("vector", "SE", s_ab)
                pr.op("vector", lambda e: e.tensor_mul(out=stub16[:], in0=stubA[:], in1=stubA[:]))
                pr.drain("vector")
                pr.op("vector", lambda e: e.tensor_sub(out=stubB[:], in0=stubB[:], in1=stub16[:].bitcast(BF16) if False else stub16[:]))
                v_var = pr.fence("vector", "VE")
                pr.wait("scalar", "VE", v_var)
                pr.op("scalar", lambda e: e.activation(out=stubB[:], in_=stubB[:], func=AF.Ln, bias=CEPSr))
                pr.drain("scalar")
                pr.op("scalar", lambda e: e.activation(out=stubB[:], in_=stubB[:], func=AF.Exp, scale=CNHr))
                pr.drain("scalar")
                pr.op("scalar", lambda e: e.copy(out=stub16[:], in_=stubA[:]))
                s_m = pr.fence("scalar", "SE")
                pr.wait("sync", "SE", s_m)
                pr.op("sync", lambda e, tc=tc: e.dma_start(
                    out=rows16[33:34, tc * 512:(tc + 1) * 512], in_=stub16[:]), sem="SQ", n=16)
                pr.wait("scalar", "SQ", pr.cnt["SQ"])
                pr.op("scalar", lambda e: e.copy(out=stub16[:], in_=stubB[:]))
                s_r = pr.fence("scalar", "SE")
                pr.wait("sync", "SE", s_r)
                v_dma_prev = pr.op("sync", lambda e, tc=tc: e.dma_start(
                    out=rows16[34:35, tc * 512:(tc + 1) * 512], in_=stub16[:]), sem="SQ", n=16)
            pr.wait("sync", "SQ", v_dma_prev)
            pr.op("sync", lambda e: e.dma_start(
                out=mu2bc[:], in_=rows16[33:34, :].partition_broadcast(128)), sem="SQ", n=16)
            v_bc2 = pr.op("sync", lambda e: e.dma_start(
                out=rs2bc[:], in_=rows16[34:35, :].partition_broadcast(128)), sem="SQ", n=16)
            pr.wait("vector", "SQ", v_bc2)
            for kc in range(NKC):
                pr.op("vector", lambda e, kc=kc: e.tensor_sub(
                    out=n2T[kc][:], in0=hs16[kc][:], in1=mu2bc[:]))
                pr.drain("vector")
                pr.op("vector", lambda e, kc=kc: e.tensor_mul(
                    out=n2T[kc][:], in0=n2T[kc][:], in1=rs2bc[:]))
                pr.drain("vector")
                pr.op("vector", lambda e, kc=kc: e.tensor_scalar(
                    out=n2T[kc][:], in0=n2T[kc][:], scalar1=_col(ln2w, kc), scalar2=_col(ln2b, kc),
                    op0=ALU.mult, op1=ALU.add))
            v_n2 = pr.fence("vector", "VE")
            pr.barrier()
        g2_scope = contextlib.ExitStack()
        sbg2 = lambda nm, shape, dt: g2_scope.enter_context(nc.sbuf_tensor(nm, shape, dt, side="right"))
        s_rw2 = gating(ap["sim2"], ap["gates2"], n2T, rwB, sbg2, "g2", v_n2)
        pr.barrier()
        g2_scope.close()

        dbg_dump("n2", n2T, pr, root)
        dbg_dump("rw2rows", [rwB], pr, root)
        pr.barrier()
        p5_ctx.close()
        if upto <= 5:
            _final_wait(pr)
            pr.finalize()
            return nc, dbg_out

        p6_ctx = contextlib.ExitStack()
        sb6 = lambda nm, shape, dt: p6_ctx.enter_context(nc.sbuf_tensor(nm, shape, dt, side="right"))
        rw2bc8 = [sb6(f"p6_rw2bc_{i}", [128, 512], BF16) for i in range(E)]
        w2sb = [sb6(f"p6_w2sb_{i}", [128, 8 * 512], BF16) for i in range(2)]
        geebuf = [sb6(f"p6_ge_{i}", [128, 8 * 512], BF16) for i in range(2)]
        g16b = sb6("p6_g16", [128, 512], BF16)
        F16t = [sb6(f"p6_F_{i}", [128, 512], BF16) for i in range(4)]
        xre = [sb6(f"p6_xre_{i}", [128, 512], F32) for i in range(2)]
        outb = [sb6(f"p6_outb_{i}", [128, 512], F32) for i in range(2)]
        w1cast = lambda par, kc: xnT[par * 4 + kc // 2][:, (kc % 2) * II:(kc % 2 + 1) * II]
        w1sl = lambda par, kc, mi: xnT[par * 4 + kc // 2][:, (kc % 2) * II + mi * 128:(kc % 2) * II + (mi + 1) * 128]
        gesl = lambda par, ki: geebuf[par][:, ki * 512:(ki + 1) * 512]
        w2sl = lambda par, ki: w2sb[par][:, ki * 512:(ki + 1) * 512]
        with contextlib.ExitStack() as ps7:
            mo_ps = ps7.enter_context(nc.psum_tensor("p6_mo", [128, 2048], F32))
            in_ps = [ps7.enter_context(nc.psum_tensor(f"p6_in_{i}", [128, 512], F32)) for i in range(2)]
            rb_ps = ps7.enter_context(nc.psum_tensor("p6_rb", [128, 512], F32))
            ftr_ps = ps7.enter_context(nc.psum_tensor("p6_ftr", [128, 128], BF16))
            w1_pe2 = [0, 0]
            w1_ld = [0, 0]
            w2_ld = [0, 0]
            w2_pe2 = [0, 0]
            ge_pe2 = [0, 0]
            ge_dma2 = [0, 0]
            inps_ev = [0, 0]
            silu_ev = 0
            f_pe = 0
            rbq_used = 0
            out_dma = [0, 0]
            xre_used = [0, 0]
            v_ge2 = [0, 0]

            def w1_load(tc, ee):
                par = ee % 2
                pr.wait("sync", "PE", w1_pe2[par])
                last = 0
                for tl in range(4):
                    last = pr.op("sync", lambda e, ee=ee, tl=tl, par=par: e.dma_start(
                        out=xnT[par * 4 + tl][:].rearrange("p (two i) -> p two i", i=II),
                        in_=w1b_scr[ee][tl * 256:(tl + 1) * 256, :].rearrange("(two p) i -> p two i", p=128)), sem="SQ", n=16)
                w1_ld[par] = last

            def w2_load(ph, ee):
                par = ee % 2
                pr.wait("scalar", "PE", w2_pe2[par])
                w2_ld[par] = pr.op("scalar", lambda e, ee=ee, ph=ph, par=par: e.dma_start(
                    out=w2sb[par][:].rearrange("p (ki c) -> p ki c", c=512),
                    in_=w2b_scr[ee][:, ph * 512:(ph + 1) * 512].rearrange("(ki p) c -> p ki c", p=128)), sem="AQ", n=16)

            def ge_load(tc, ee):
                par = ee % 2
                pr.wait("scalar", "PE", ge_pe2[par])
                ge_dma2[par] = pr.op("scalar", lambda e, tc=tc, ee=ee, par=par: e.dma_start(
                    out=geebuf[par][:], in_=ge_scr[tc, ee]), sem="AQ", n=16)

            for ph in range(2):
                for tc in range(4):
                    pr.wait("tensor", "SE", silu_ev)
                    for mcr in range(4):
                        pr.op("tensor", lambda e, ph=ph, tc=tc, mcr=mcr: e.matmul(
                            out=mo_ps[:, mcr * 512:(mcr + 1) * 512], lhsT=ident16[:],
                            rhs=hs16[ph * 4 + mcr][:, tc * 512:(tc + 1) * 512],
                            start=True, stop=False))
                    w2_last = 0
                    if ph == 0:
                        w1_load(tc, 0)
                        w2_load(ph, 0)
                        pr.wait("tensor", "VE", rbq_used)
                        s_rb = 0
                        for ee in range(E):
                            pr.wait("tensor", "SE", s_rb)
                            pe_rb = pr.op("tensor", lambda e, ee=ee, tc=tc: e.matmul(
                                out=rb_ps[:, (ee % 2) * 256:(ee % 2) * 256 + 256][:, 0:256] if False else rb_ps[:],
                                lhsT=sel8[:, ee * 128:(ee + 1) * 128],
                                rhs=rwB[0:8, tc * 512:(tc + 1) * 512], start=True, stop=True), sem="PE")
                            pr.wait("scalar", "PE", pe_rb)
                            s_rb = pr.op("scalar", lambda e, ee=ee: e.copy(out=rw2bc8[ee][:], in_=rb_ps[:]), sem="SE")
                        s_rb_all = s_rb
                    else:
                        ge_load(tc, 0)
                        w2_load(ph, 0)
                    for ee in range(E):
                        par = ee % 2
                        if ph == 0:
                            if ee + 1 < E:
                                w1_load(tc, ee + 1)
                                w2_load(ph, ee + 1)
                            pr.wait("tensor", "SQ", w1_ld[par])
                            pr.wait("tensor", "AQ", ge_dma2[par])
                            pr.wait("tensor", "PE", ge_pe2[par])
                            w1_last = 0
                            for mi in range(NKC):
                                pb = mi % 2
                                pr.wait("tensor", "SE", inps_ev[pb])
                                for kc in range(NKC):
                                    w1_last = pr.op("tensor", lambda e, mi=mi, kc=kc, tc=tc, pb=pb, par=par: e.matmul(
                                        out=in_ps[pb][:], lhsT=w1sl(par, kc, mi),
                                        rhs=n2T[kc][:, tc * 512:(tc + 1) * 512],
                                        start=(kc == 0), stop=(kc == NKC - 1)),
                                        sem="PE" if kc == NKC - 1 else None)
                                pr.wait("scalar", "PE", w1_last)
                                s_ge = pr.op("scalar", lambda e, mi=mi, pb=pb, par=par: e.activation(
                                    out=gesl(par, mi), in_=in_ps[pb][:], func=AF.Gelu), sem="SE")
                                inps_ev[pb] = s_ge
                                pr.wait("vector", "SE", max(s_ge, s_rb_all))
                                pr.op("vector", lambda e, mi=mi, par=par, ee=ee: e.tensor_mul(
                                    out=gesl(par, mi), in0=gesl(par, mi), in1=rw2bc8[ee][:]))
                            w1_pe2[par] = w1_last
                            v_ge2[par] = pr.fence("vector", "VE")
                            rbq_used = v_ge2[par]
                            pr.wait("scalar", "VE", v_ge2[par])
                            ge_dma2[par] = pr.op("scalar", lambda e, tc=tc, ee=ee, par=par: e.dma_start(
                                out=ge_scr[tc, ee], in_=geebuf[par][:]), sem="AQ", n=16)
                        else:
                            if ee + 1 < E:
                                ge_load(tc, ee + 1)
                                w2_load(ph, ee + 1)
                        pr.wait("tensor", "AQ", w2_ld[par])
                        if ph == 0:
                            pr.wait("tensor", "VE", v_ge2[par])
                        else:
                            pr.wait("tensor", "AQ", ge_dma2[par])
                        for mcr in range(4):
                            for ki in range(NKC):
                                w2_last = pr.op("tensor", lambda e, mcr=mcr, ki=ki, ee=ee, par=par: e.matmul(
                                    out=mo_ps[:, mcr * 512 + 0:mcr * 512 + 512],
                                    lhsT=w2sl(par, ki)[:, mcr * 128:(mcr + 1) * 128],
                                    rhs=gesl(par, ki),
                                    start=False, stop=(ee == E - 1 and ki == NKC - 1)),
                                    sem="PE" if (ki == NKC - 1 and mcr == 3) else None)
                        ge_pe2[par] = w2_last
                        w2_pe2[par] = w2_last
                    pr.wait("scalar", "PE", w2_last)
                    v_F = 0
                    for mcr in range(4):
                        mc = ph * 4 + mcr
                        pr.wait("scalar", "PE", f_pe)
                        pr.wait("scalar", "VE", v_F)
                        s_g = pr.op("scalar", lambda e, mcr=mcr: e.activation(
                            out=g16b[:], in_=mo_ps[:, mcr * 512:(mcr + 1) * 512], func=AF.Silu), sem="SE")
                        silu_ev = s_g
                        pr.wait("vector", "SE", s_g)
                        pr.op("vector", lambda e, mcr=mcr, mc=mc, tc=tc: e.tensor_mul(
                            out=F16t[mcr][:], in0=g16b[:], in1=sout16[mc][:, tc * 512:(tc + 1) * 512]))
                        v_F = pr.fence("vector", "VE")
                    for tb in range(4):
                        tt = tc * 4 + tb
                        xb2 = tb % 2
                        pr.wait("sync", "VE", xre_used[xb2])
                        v_x = pr.op("sync", lambda e, tt=tt, ph=ph, xb2=xb2: e.dma_start(
                            out=xre[xb2][:], in_=ap["x"][tt * 128:(tt + 1) * 128, ph * 512:(ph + 1) * 512]), sem="SQ", n=16)
                        pr.wait("vector", "SQ", max(v_x, out_dma[xb2]))
                        for mcr in range(4):
                            pr.wait("tensor", "VE", v_F)
                            pe_f = pr.op("tensor", lambda e, mcr=mcr, tb=tb: e.transpose(
                                out=ftr_ps[:], in_=F16t[mcr][:, tb * 128:(tb + 1) * 128], identity=ident16[:]), sem="PE")
                            pr.wait("vector", "PE", pe_f)
                            v_o = pr.op("vector", lambda e, mcr=mcr, xb2=xb2: e.tensor_add(
                                out=outb[xb2][:, mcr * 128:(mcr + 1) * 128], in0=ftr_ps[:],
                                in1=xre[xb2][:, mcr * 128:(mcr + 1) * 128]), sem="VE")
                            pr.wait("tensor", "VE", v_o)
                        xre_used[xb2] = v_o
                        f_pe = pr.cnt["PE"]
                        pr.wait("sync", "VE", v_o)
                        out_dma[xb2] = pr.op("sync", lambda e, tt=tt, ph=ph, xb2=xb2: e.dma_start(
                            out=ap["out"][tt * 128:(tt + 1) * 128, ph * 512:(ph + 1) * 512],
                            in_=outb[xb2][:]), sem="SQ", n=16)
                pr.barrier()
        p6_ctx.close()

        _final_wait(pr)
        pr.finalize()
        return nc, dbg_out


def _final_wait(pr):
    pr.wait("sync", "SQ", pr.cnt["SQ"])
    pr.wait("sync", "AQ", pr.cnt["AQ"])


def _host_consts():
    ident = np.eye(128, dtype=np.float32)
    ones = np.ones((128, 128), dtype=np.float32)
    sel8 = np.zeros((E, E * 128), dtype=np.float32)
    for e in range(E):
        sel8[e, e * 128:(e + 1) * 128] = 1.0
    dmask = np.where(np.arange(128)[None, :] <= np.arange(128)[:, None],
                     0.0, NEGBIG).astype(np.float32)
    return {
        "ident16": ident.astype(ml_dtypes.bfloat16),
        "ident32": ident,
        "ones16": ones.astype(ml_dtypes.bfloat16),
        "ones32": ones,
        "sel8": sel8.astype(ml_dtypes.bfloat16),
        "dmask": dmask,
    }


def _in_map(core_x, inputs):
    m = {
        "x": np.ascontiguousarray(core_x),
        "w_in": inputs["L0_attn_W_in"],
        "conv_w": np.ascontiguousarray(np.asarray(inputs["L0_attn_conv_w"])[:, 0, :]),
        "conv_b": np.asarray(inputs["L0_attn_conv_b"])[None, :],
        "a_log": inputs["L0_attn_A_log"],
        "w_x": inputs["L0_attn_W_x"],
        "w_out": inputs["L0_attn_W_out"],
        "ln1_w": np.asarray(inputs["L0_ffn_ln1_w"])[None, :],
        "ln1_b": np.asarray(inputs["L0_ffn_ln1_b"])[None, :],
        "ln2_w": np.asarray(inputs["L0_ffn_ln2_w"])[None, :],
        "ln2_b": np.asarray(inputs["L0_ffn_ln2_b"])[None, :],
        "sim1": inputs["L0_ffn_smha_sim"],
        "gates1": np.asarray(inputs["L0_ffn_smha_gates"])[None, :],
        "qp": inputs["L0_ffn_q_proj"],
        "kp": inputs["L0_ffn_k_proj"],
        "vp": inputs["L0_ffn_v_proj"],
        "op": inputs["L0_ffn_o_proj"],
        "sim2": inputs["L0_ffn_moe_sim"],
        "gates2": np.asarray(inputs["L0_ffn_moe_gates"])[None, :],
        "w1": inputs["L0_ffn_w1"],
        "w2": inputs["L0_ffn_w2"],
    }
    m = {k: np.ascontiguousarray(np.asarray(v, dtype=np.float32)) for k, v in m.items()}
    m.update(_host_consts())
    return m


def kernel(**inputs):
    x = np.asarray(inputs["L0_attn_x"], dtype=np.float32)
    nc, dbg_out = build(upto=UPTO, dbg=DBG)
    in_maps = [_in_map(x[b], inputs) for b in range(8)]
    res = run_bass_kernel_spmd(nc, in_maps, core_ids=list(range(8)))
    out = np.stack([np.asarray(res.results[b]["out"], dtype=np.float32) for b in range(8)], axis=0)
    if DBG:
        kernel._dbg = res
    return out


# version 10 (final)
# speedup vs baseline: 1.0922x; 1.0922x over previous
import contextlib
import numpy as np
import ml_dtypes
import concourse.bass as bass
import concourse.mybir as mybir
from concourse.bass_utils import run_bass_kernel_spmd

F32 = mybir.dt.float32
BF16 = mybir.dt.bfloat16
AF = mybir.ActivationFunctionType
ALU = mybir.AluOpType
AX = mybir.AxisListType

T, C, DI, DS, H, E, II = 2048, 1024, 2048, 16, 128, 8, 1024
NKC, NDD, NTT = C // 128, DI // 128, T // 128
EPS = 1e-5
NEGBIG = -30000.0
RS = 1.0 / np.sqrt(128.0)

UPTO = 99
DBG = False

ENGS = ("sync", "scalar", "vector", "tensor", "gpsimd")


class Prog:
    def __init__(self, nc, ctx):
        self.nc = nc
        self.streams = {e: [] for e in ENGS}
        self.sems = {}
        self.cnt = {}
        for name in ("SQ", "AQ", "PE", "VE", "SE", "BR", "PO"):
            self.sems[name] = ctx.enter_context(nc.semaphore(name))
            self.cnt[name] = 0

    def em(self, eng, fn):
        self.streams[eng].append(fn)

    def wait(self, eng, sem, val):
        if val is None or val <= 0:
            return
        s = self.sems[sem]
        self.em(eng, lambda e, s=s, v=val: e.wait_ge(s, v))

    def op(self, eng, fn, sem=None, n=1):
        if sem is None:
            self.em(eng, fn)
            return None
        s = self.sems[sem]
        self.cnt[sem] += n
        v = self.cnt[sem]
        self.em(eng, lambda e, fn=fn, s=s, n=n: fn(e).then_inc(s, n))
        return v

    def fence(self, eng, sem):
        s = self.sems[sem]
        self.cnt[sem] += 1
        v = self.cnt[sem]
        self.em(eng, lambda e, s=s: e.drain().then_inc(s, 1))
        return v

    def drain(self, eng):
        self.em(eng, lambda e: e.drain())

    def barrier(self):
        self.wait("sync", "SQ", self.cnt["SQ"])
        self.wait("scalar", "AQ", self.cnt["AQ"])
        s = self.sems["BR"]
        engs = ("sync", "scalar", "vector", "tensor")
        self.cnt["BR"] += len(engs)
        tgt = self.cnt["BR"]
        for eng in engs:
            self.em(eng, lambda e, s=s: e.drain().then_inc(s, 1))
        for eng in engs:
            self.em(eng, lambda e, s=s, v=tgt: e.wait_ge(s, v))

    def finalize(self):
        nc = self.nc
        streams = self.streams
        with nc.Block() as block:
            @block.sync
            def _(e):
                for fn in streams["sync"]:
                    fn(e)

            @block.scalar
            def _(e):
                for fn in streams["scalar"]:
                    fn(e)

            @block.vector
            def _(e):
                for fn in streams["vector"]:
                    fn(e)

            @block.tensor
            def _(e):
                for fn in streams["tensor"]:
                    fn(e)

            @block.gpsimd
            def _(e):
                for fn in streams["gpsimd"]:
                    fn(e)


def _col(tile, j):
    return tile[:, j:j + 1]


def build(upto=99, dbg=False):
    nc = bass.Bass("TRN2", target_bir_lowering=False)
    ap = {}
    ap["x"] = nc.dram_tensor("x", [T, C], F32, kind="ExternalInput").ap()
    ap["w_in"] = nc.dram_tensor("w_in", [2 * DI, C], F32, kind="ExternalInput").ap()
    ap["conv_w"] = nc.dram_tensor("conv_w", [DI, 4], F32, kind="ExternalInput").ap()
    ap["conv_b"] = nc.dram_tensor("conv_b", [1, DI], F32, kind="ExternalInput").ap()
    ap["a_log"] = nc.dram_tensor("a_log", [DI, DS], F32, kind="ExternalInput").ap()
    ap["w_x"] = nc.dram_tensor("w_x", [2 * DS + 1, DI], F32, kind="ExternalInput").ap()
    ap["w_out"] = nc.dram_tensor("w_out", [C, DI], F32, kind="ExternalInput").ap()
    ap["ln1_w"] = nc.dram_tensor("ln1_w", [1, C], F32, kind="ExternalInput").ap()
    ap["ln1_b"] = nc.dram_tensor("ln1_b", [1, C], F32, kind="ExternalInput").ap()
    ap["ln2_w"] = nc.dram_tensor("ln2_w", [1, C], F32, kind="ExternalInput").ap()
    ap["ln2_b"] = nc.dram_tensor("ln2_b", [1, C], F32, kind="ExternalInput").ap()
    ap["sim1"] = nc.dram_tensor("sim1", [C, E], F32, kind="ExternalInput").ap()
    ap["gates1"] = nc.dram_tensor("gates1", [1, E], F32, kind="ExternalInput").ap()
    ap["qp"] = nc.dram_tensor("qp", [E, C, H], F32, kind="ExternalInput").ap()
    ap["kp"] = nc.dram_tensor("kp", [E, C, H], F32, kind="ExternalInput").ap()
    ap["vp"] = nc.dram_tensor("vp", [E, C, H], F32, kind="ExternalInput").ap()
    ap["op"] = nc.dram_tensor("op", [E, H, C], F32, kind="ExternalInput").ap()
    ap["sim2"] = nc.dram_tensor("sim2", [C, E], F32, kind="ExternalInput").ap()
    ap["gates2"] = nc.dram_tensor("gates2", [1, E], F32, kind="ExternalInput").ap()
    ap["w1"] = nc.dram_tensor("w1", [E, C, II], F32, kind="ExternalInput").ap()
    ap["w2"] = nc.dram_tensor("w2", [E, II, C], F32, kind="ExternalInput").ap()
    ap["ident16"] = nc.dram_tensor("ident16", [128, 128], BF16, kind="ExternalInput").ap()
    ap["ident32"] = nc.dram_tensor("ident32", [128, 128], F32, kind="ExternalInput").ap()
    ap["ones16"] = nc.dram_tensor("ones16", [128, 128], BF16, kind="ExternalInput").ap()
    ap["ones32"] = nc.dram_tensor("ones32", [128, 128], F32, kind="ExternalInput").ap()
    ap["sel8"] = nc.dram_tensor("sel8", [E, E * 128], BF16, kind="ExternalInput").ap()
    ap["dmask"] = nc.dram_tensor("dmask", [128, 128], F32, kind="ExternalInput").ap()
    ap["out"] = nc.dram_tensor("out", [T, C], F32, kind="ExternalOutput").ap()
    rows16 = nc.dram_tensor("rows16_scr", [40, T], BF16, kind="Internal").ap()
    rows32 = nc.dram_tensor("rows32_scr", [4, T], F32, kind="Internal").ap()
    c2scr = nc.dram_tensor("c2_scr", [NTT, 128], F32, kind="Internal").ap()
    ge_scr = nc.dram_tensor("ge_scr", [4, E, 128, 8 * 512], BF16, kind="Internal").ap()
    sz_scr = nc.dram_tensor("sz_scr", [DI, T], BF16, kind="Internal").ap()
    xn32_scr = nc.dram_tensor("xn32_scr", [T, C], F32, kind="Internal").ap()
    w1b_scr = nc.dram_tensor("w1b_scr", [E, C, II], BF16, kind="Internal").ap()
    w2b_scr = nc.dram_tensor("w2b_scr", [E, II, C], BF16, kind="Internal").ap()
    qkv_scr = nc.dram_tensor("qkv_scr", [3, E, 128, NKC * 128], BF16, kind="Internal").ap()

    dbg_out = {}

    def dbg_dump(name, tiles, pr, root):
        if not dbg:
            return
        n = len(tiles)
        p, f = tiles[0].shape[0], tiles[0].shape[1]
        dt = tiles[0].dtype
        o = nc.dram_tensor("dbg_" + name, [n * p, f], dt, kind="ExternalOutput").ap()
        dbg_out[name] = (n, p, f)
        for i, tl in enumerate(tiles):
            pr.op("sync", lambda e, o=o, i=i, tl=tl, p=p: e.dma_start(
                out=o[i * p:(i + 1) * p, :], in_=tl[:]), sem="SQ", n=16)

    root = contextlib.ExitStack()
    with root:
        pr = Prog(nc, root)
        sb = lambda name, shape, dt: root.enter_context(nc.sbuf_tensor(name, shape, dt))

        ident16 = sb("c_ident16", [128, 128], BF16)
        ident32 = sb("c_ident32", [128, 128], F32)
        ones16 = sb("c_ones16", [128, 128], BF16)
        ones32 = sb("c_ones32", [128, 128], F32)
        sel8 = sb("c_sel8", [E, E * 128], BF16)
        dmask = sb("c_dmask", [128, 128], F32)
        ln1w = sb("c_ln1w", [128, NKC], F32)
        ln1b = sb("c_ln1b", [128, NKC], F32)
        ln2w = sb("c_ln2w", [128, NKC], F32)
        ln2b = sb("c_ln2b", [128, NKC], F32)
        cbias = sb("c_cbias", [128, NDD], F32)
        cw = sb("c_cw", [128, 4 * NDD], F32)
        A_sb = sb("c_A", [128, DS * NDD], F32)

        for dst, src in ((ident16, "ident16"), (ident32, "ident32"),
                         (ones16, "ones16"), (ones32, "ones32"),
                         (sel8, "sel8"), (dmask, "dmask")):
            pr.op("sync", lambda e, d=dst, s=ap[src]: e.dma_start(out=d[:], in_=s), sem="SQ", n=16)
        for dst, src in ((ln1w, "ln1_w"), (ln1b, "ln1_b"), (ln2w, "ln2_w"), (ln2b, "ln2_b")):
            pr.op("sync", lambda e, d=dst, s=ap[src]: e.dma_start(
                out=d[:], in_=s.rearrange("one (kc p) -> (one p) kc", p=128),
                allow_slow_non_contiguous=True), sem="SQ", n=16)
        pr.op("sync", lambda e, d=cbias, s=ap["conv_b"]: e.dma_start(
            out=d[:], in_=s.rearrange("one (dd p) -> (one p) dd", p=128),
            allow_slow_non_contiguous=True), sem="SQ", n=16)
        for dd in range(NDD):
            pr.op("sync", lambda e, d=cw, s=ap["conv_w"], dd=dd: e.dma_start(
                out=d[:, dd * 4:(dd + 1) * 4], in_=s[dd * 128:(dd + 1) * 128, :]), sem="SQ", n=16)
            pr.op("sync", lambda e, d=A_sb, s=ap["a_log"], dd=dd: e.dma_start(
                out=d[:, dd * DS:(dd + 1) * DS], in_=s[dd * 128:(dd + 1) * 128, :]), sem="SQ", n=16)
        v_const = pr.cnt["SQ"]
        cst = sb("c_cst", [128, 8], F32)
        for j, val in ((0, float(EPS)), (1, -0.5), (2, -1.0), (3, 1.0)):
            pr.op("vector", lambda e, j=j, val=val: e.memset(cst[:, j:j + 1], val))
        v_cst = pr.op("vector", lambda e: e.memset(cst[:, 4:5], float(RS)), sem="VE")
        pr.wait("scalar", "VE", v_cst)
        for j, val in ((0, float(EPS)), (1, -0.5), (2, -1.0), (3, 1.0), (4, float(RS))):
            nc.const_aps.register(F32, val, cst[:, j:j + 1])
        CEPS = _col(cst, 0)
        CEPSr = cst[0:1, 0:1]
        CNHr = cst[0:1, 1:2]
        CN1r = cst[0:1, 2:3]
        C1r = cst[0:1, 3:4]
        CNH = _col(cst, 1)
        CN1 = _col(cst, 2)
        C1 = _col(cst, 3)
        CRS = _col(cst, 4)
        pr.wait("scalar", "SQ", v_const)
        se_a = pr.op("scalar", lambda e, d=A_sb: e.activation(out=d[:], in_=d[:], func=AF.Exp), sem="SE")
        pr.wait("vector", "SE", se_a)
        ve_a = pr.op("vector", lambda e, d=A_sb: e.tensor_scalar_mul(out=d[:], in0=d[:], scalar1=-1.0), sem="VE")

        xnT = [sb(f"xnT_{k}", [128, T], BF16) for k in range(NKC)]

        p0_ctx = contextlib.ExitStack()
        sb0 = lambda name, shape, dt: p0_ctx.enter_context(nc.sbuf_tensor(name, shape, dt, side="right"))
        xt = [sb0(f"p0_x_{i}", [128, C], F32) for i in range(2)]
        xn32b = [sb0(f"p0_xn32_{i}", [128, C], F32) for i in range(2)]
        xn32_dma = [0, 0]
        xn16 = [sb0(f"p0_xn_{i}", [128, C], BF16) for i in range(2)]
        stat = sb0("p0_stat", [128, 16], F32)
        sqb = sb0("p0_sq", [128, C], F32)
        with nc.psum_tensor("p0_tr0", [128, 128], BF16) as tr_ps0, \
             nc.psum_tensor("p0_tr1", [128, 128], BF16) as tr_ps1:
            tr_ps2 = [tr_ps0, tr_ps1]
            tr_ev = [0, 0]
            last_xn_evict = [0, 0]
            last_pe_for_buf = [0, 0]
            for tt in range(NTT):
                b = tt % 2
                pr.wait("sync", "VE", last_xn_evict[b])
                v_ld = pr.op("sync", lambda e, d=xt[b], s=ap["x"], tt=tt: e.dma_start(
                    out=d[:], in_=s[tt * 128:(tt + 1) * 128, :]), sem="SQ", n=16)
                pr.wait("vector", "SQ", v_ld)
                pr.op("vector", lambda e, x=xt[b]: e.tensor_mul(out=sqb[:], in0=x[:], in1=x[:]))
                pr.op("vector", lambda e, x=xt[b]: e.reduce_sum(out=_col(stat, 0), in_=x[:], axis=AX.X))
                pr.op("vector", lambda e: e.reduce_sum(out=_col(stat, 1), in_=sqb[:], axis=AX.X))
                pr.drain("vector")
                pr.op("vector", lambda e: e.tensor_scalar_mul(out=_col(stat, 2), in0=_col(stat, 0), scalar1=1.0 / C))
                pr.op("vector", lambda e: e.tensor_scalar_mul(out=_col(stat, 3), in0=_col(stat, 1), scalar1=1.0 / C))
                pr.drain("vector")
                pr.op("vector", lambda e: e.tensor_mul(out=_col(stat, 4), in0=_col(stat, 2), in1=_col(stat, 2)))
                pr.drain("vector")
                pr.op("vector", lambda e: e.tensor_sub(out=_col(stat, 5), in0=_col(stat, 3), in1=_col(stat, 4)))
                v_va = pr.fence("vector", "VE")
                pr.wait("scalar", "VE", v_va)
                pr.op("scalar", lambda e: e.activation(out=_col(stat, 6), in_=_col(stat, 5), func=AF.Ln, bias=CEPS))
                pr.drain("scalar")
                pr.op("scalar", lambda e: e.activation(out=_col(stat, 6), in_=_col(stat, 6), func=AF.Exp, scale=CNH))
                s_rstd = pr.fence("scalar", "SE")
                pr.wait("vector", "SE", s_rstd)
                pr.op("vector", lambda e: e.tensor_mul(out=_col(stat, 7), in0=_col(stat, 2), in1=_col(stat, 6)))
                pr.drain("vector")
                pr.op("vector", lambda e: e.tensor_scalar_mul(out=_col(stat, 7), in0=_col(stat, 7), scalar1=-1.0))
                pr.op("vector", lambda e: e.tensor_scalar_add(out=_col(stat, 8), in0=_col(stat, 5), scalar1=float(EPS)))
                pr.drain("vector")
                pr.op("vector", lambda e: e.reciprocal(out=_col(stat, 9), in_=_col(stat, 8)))
                pr.drain("vector")
                pr.op("vector", lambda e: e.tensor_mul(out=_col(stat, 10), in0=_col(stat, 5), in1=_col(stat, 9)))
                v_c2 = pr.fence("vector", "VE")
                pr.wait("scalar", "VE", v_c2)
                pr.wait("vector", "PE", last_pe_for_buf[b])
                s_xn = pr.op("vector", lambda e, x=xt[b], o=xn16[b]: e.tensor_scalar(
                    out=o[:], in0=x[:], scalar1=_col(stat, 6), scalar2=_col(stat, 7),
                    op0=ALU.mult, op1=ALU.add), sem="VE")
                pr.wait("vector", "SQ", xn32_dma[b])
                s_xn32 = pr.op("vector", lambda e, x=xt[b], o=xn32b[b]: e.tensor_scalar(
                    out=o[:], in0=x[:], scalar1=_col(stat, 6), scalar2=_col(stat, 7),
                    op0=ALU.mult, op1=ALU.add), sem="VE")
                pr.wait("sync", "VE", s_xn32)
                xn32_dma[b] = pr.op("sync", lambda e, b=b, tt=tt: e.dma_start(
                    out=xn32_scr[tt * 128:(tt + 1) * 128, :], in_=xn32b[b][:]), sem="SQ", n=16)
                last_xn_evict[b] = s_xn32
                pr.op("scalar", lambda e: e.activation(out=_col(stat, 11), in_=_col(stat, 10), func=AF.Ln, bias=CEPS))
                pr.drain("scalar")
                pr.op("scalar", lambda e: e.activation(out=_col(stat, 11), in_=_col(stat, 11), func=AF.Exp, scale=CNH))
                s_c2 = pr.fence("scalar", "SE")
                pe_last = 0
                pr.wait("tensor", "VE", s_xn)
                for kc in range(NKC):
                    pb = kc % 2
                    pr.wait("tensor", "SE", tr_ev[pb])
                    pe_v = pr.op("tensor", lambda e, b=b, kc=kc, pb=pb: e.transpose(
                        out=tr_ps2[pb][:], in_=xn16[b][:, kc * 128:(kc + 1) * 128], identity=ident16[:]), sem="PE")
                    pr.wait("scalar", "PE", pe_v)
                    tr_ev[pb] = pr.op("scalar", lambda e, kc=kc, tt=tt, pb=pb: e.copy(
                        out=xnT[kc][:, tt * 128:(tt + 1) * 128], in_=tr_ps2[pb][:]), sem="SE")
                    pe_last = pe_v
                last_pe_for_buf[b] = pe_last
                pr.wait("sync", "SE", s_c2)
                v_c2dma = pr.op("sync", lambda e, tt=tt: e.dma_start(
                    out=c2scr[tt:tt + 1, :], in_=_col(stat, 11),
                    allow_slow_non_contiguous=True), sem="SQ", n=16)
                pr.wait("scalar", "SQ", v_c2dma)
            pr.barrier()
        c2row = sb0("c2row", [1, T], F32)
        c2row16 = sb("c2row16", [1, T], BF16)
        c2bc16 = sb("c2bc16", [128, T], BF16)
        v_c2r = pr.op("sync", lambda e: e.dma_start(
            out=c2row[:], in_=c2scr.rearrange("a b -> (a b)").unsqueeze(0)), sem="SQ", n=16)
        pr.wait("scalar", "SQ", v_c2r)
        s_c216 = pr.op("scalar", lambda e: e.copy(out=c2row16[:], in_=c2row[:]), sem="SE")
        pr.wait("sync", "SE", s_c216)
        v_c2w = pr.op("sync", lambda e: e.dma_start(out=rows16[32:33, :], in_=c2row16[:]), sem="SQ", n=16)
        pr.wait("sync", "SQ", v_c2w)
        v_c2b = pr.op("sync", lambda e: e.dma_start(
            out=c2bc16[:], in_=rows16[32:33, :].partition_broadcast(128)), sem="SQ", n=16)

        dbg_dump("xnT", xnT, pr, root)
        dbg_dump("c2row", [c2row], pr, root)
        pr.barrier()
        p0_ctx.close()

        if upto <= 0:
            _final_wait(pr)
            pr.finalize()
            return nc, dbg_out

        xc16 = [sb(f"xc_{d}", [128, T], BF16) for d in range(NDD)]
        p1_rows = contextlib.ExitStack()
        sbr = lambda name, shape, dt: p1_rows.enter_context(nc.sbuf_tensor(name, shape, dt, side="right"))
        drow = sbr("p1_drow", [1, T], F32)
        brow = sbr("p1_brow", [DS, T], F32)
        crow = sbr("p1_crow", [DS, T], F32)
        db16 = sbr("p1_db16", [DS, T], BF16)
        c16r = sbr("p1_c16r", [DS, T], BF16)
        dbc16 = sbr("p1_dbc16", [DS, T], F32)
        trow = sbr("p1_trow", [1, T], F32)
        p1_ctx = contextlib.ExitStack()
        sbp = lambda name, shape, dt: p1_ctx.enter_context(nc.sbuf_tensor(name, shape, dt, side="right"))
        wxT = [sbp(f"wxT_{d}", [128, 33], BF16) for d in range(NDD)]
        wnat32 = [sbp(f"p1_wn32_{i}", [128, C], F32) for i in range(2)]
        wnat16 = [sbp(f"p1_wn16_{i}", [128, C], BF16) for i in range(2)]
        wT16 = [[sbp(f"p1_wT{i}_{k}", [128, 128], BF16) for k in range(NKC)] for i in range(2)]
        xp16 = sbp("p1_xp", [128, T], BF16)
        wx32 = sbp("p1_wx32", [33, DI], F32)
        wx16 = sbp("p1_wx16", [33, DI], BF16)
        dgbuf = [sbp(f"p1_dg_{k}", [128, 128], BF16) for k in range(4)]
        zstg = [sbp(f"p1_zstg_{i}", [128, 512], BF16) for i in range(2)]
        zstg_dma = [0, 0]

        with contextlib.ExitStack() as psctx:
            ps = lambda name, shape, dt: psctx.enter_context(nc.psum_tensor(name, shape, dt))
            wtr_ps2 = [ps("p1_wtr0", [128, 128], BF16), ps("p1_wtr1", [128, 128], BF16)]
            wtr_ev = [0, 0]
            xz_ps = [ps(f"p1_xz_{i}", [128, 512], F32) for i in range(2)]
            cv_ps = [ps(f"p1_cv_{i}", [128, 512], F32) for i in range(2)]
            wxp_ps = ps("p1_wxp", [128, 33], BF16)
            v_wx = pr.op("sync", lambda e: e.dma_start(out=wx32[:], in_=ap["w_x"]), sem="SQ", n=16)
            pr.wait("scalar", "SQ", v_wx)
            s_wx = pr.op("scalar", lambda e: e.copy(out=wx16[:], in_=wx32[:]), sem="SE")
            pr.wait("tensor", "SE", s_wx)
            for dd in range(NDD):
                pe_v = pr.op("tensor", lambda e, dd=dd: e.transpose(
                    out=wxp_ps[:], in_=wx16[0:33, dd * 128:(dd + 1) * 128],
                    identity=ident16[0:33, 0:33]), sem="PE")
                pr.wait("scalar", "PE", pe_v)
                s_ev = pr.op("scalar", lambda e, dd=dd: e.copy(out=wxT[dd][:], in_=wxp_ps[:]), sem="SE")
                pr.wait("tensor", "SE", s_ev)
            cast_done = [0, 0]
            wt_done = [0, 0]
            mm_done = [0, 0]
            xz_ev = [0, 0]
            xz_mm = [0, 0]
            cv_ev = [0, 0]
            cv_mm = [0, 0]
            conv_pe_done = 0
            for me in range(2 * NDD):
                buf = me % 2
                pr.wait("sync", "SE", cast_done[buf])
                v_ld = pr.op("sync", lambda e, d=wnat32[buf], me=me: e.dma_start(
                    out=d[:], in_=ap["w_in"][me * 128:(me + 1) * 128, :]), sem="SQ", n=16)
                pr.wait("scalar", "SQ", v_ld)
                pr.wait("scalar", "PE", mm_done[buf])
                cast_done[buf] = pr.op("scalar", lambda e, b=buf: e.copy(
                    out=wnat16[b][:], in_=wnat32[b][:]), sem="SE")
                pr.wait("tensor", "SE", cast_done[buf])
                pr.wait("tensor", "PE", mm_done[buf])
                for kc in range(NKC):
                    pb = kc % 2
                    pr.wait("tensor", "SE", wtr_ev[pb])
                    pe_v = pr.op("tensor", lambda e, b=buf, kc=kc, pb=pb: e.transpose(
                        out=wtr_ps2[pb][:], in_=wnat16[b][:, kc * 128:(kc + 1) * 128], identity=ident16[:]), sem="PE")
                    pr.wait("scalar", "PE", pe_v)
                    wtr_ev[pb] = pr.op("scalar", lambda e, b=buf, kc=kc, pb=pb: e.copy(
                        out=wT16[b][kc][:], in_=wtr_ps2[pb][:]), sem="SE")
                wt_done[buf] = pr.cnt["SE"]
                pr.wait("tensor", "SE", wt_done[buf])
                last_mm = 0
                for tc in range(4):
                    pb = tc % 2
                    pr.wait("tensor", "SE", xz_ev[pb])
                    for kc in range(NKC):
                        last_mm = pr.op("tensor", lambda e, b=buf, kc=kc, tc=tc, pb=pb: e.matmul(
                            out=xz_ps[pb][:], lhsT=wT16[b][kc][:],
                            rhs=xnT[kc][:, tc * 512:(tc + 1) * 512],
                            start=(kc == 0), stop=(kc == NKC - 1)), sem="PE" if kc == NKC - 1 else None)
                    xz_mm[pb] = last_mm
                    pr.wait("scalar", "PE", xz_mm[pb])
                    if me < NDD:
                        xz_ev[pb] = pr.op("scalar", lambda e, me=me, tc=tc, pb=pb: e.copy(
                            out=xp16[:, tc * 512:(tc + 1) * 512], in_=xz_ps[pb][:]), sem="SE")
                    else:
                        zz = me - NDD
                        pr.wait("scalar", "SQ", zstg_dma[pb])
                        xz_ev[pb] = pr.op("scalar", lambda e, pb=pb: e.activation(
                            out=zstg[pb][:], in_=xz_ps[pb][:], func=AF.Silu), sem="SE")
                        pr.wait("sync", "SE", xz_ev[pb])
                        zstg_dma[pb] = pr.op("sync", lambda e, zz=zz, tc=tc, pb=pb: e.dma_start(
                            out=sz_scr[zz * 128:(zz + 1) * 128, tc * 512:(tc + 1) * 512],
                            in_=zstg[pb][:]), sem="SQ", n=16)
                mm_done[buf] = last_mm
                if me < NDD:
                    dd = me
                    pr.wait("scalar", "PE", conv_pe_done)
                    dg_done = 0
                    for k in range(4):
                        dg_done = pr.op("scalar", lambda e, dd=dd, k=k: e.mul(
                            out=dgbuf[k][:], in_=ident16[:], mul=_col(cw, dd * 4 + k)), sem="SE")
                    pr.wait("tensor", "SE", dg_done)
                    conv_last = 0
                    for tc in range(4):
                        pb = tc % 2
                        pr.wait("tensor", "SE", cv_ev[pb])
                        lo = tc * 512
                        for k in (3, 2, 1, 0):
                            sh = 3 - k
                            if tc == 0 and sh > 0:
                                conv_last = pr.op("tensor", lambda e, k=k, sh=sh, pb=pb: e.matmul(
                                    out=cv_ps[pb][:, sh:512], lhsT=dgbuf[k][:],
                                    rhs=xp16[:, 0:512 - sh],
                                    start=False, stop=(k == 0)), sem="PE" if k == 0 else None)
                            else:
                                conv_last = pr.op("tensor", lambda e, k=k, sh=sh, lo=lo, pb=pb: e.matmul(
                                    out=cv_ps[pb][:], lhsT=dgbuf[k][:],
                                    rhs=xp16[:, lo - sh:lo + 512 - sh],
                                    start=(k == 3), stop=(k == 0)), sem="PE" if k == 0 else None)
                        cv_mm[pb] = conv_last
                        pr.wait("scalar", "PE", cv_mm[pb])
                        cv_ev[pb] = pr.op("scalar", lambda e, dd=dd, tc=tc, pb=pb: e.activation(
                            out=xc16[dd][:, tc * 512:(tc + 1) * 512], in_=cv_ps[pb][:],
                            func=AF.Silu, bias=_col(cbias, dd)), sem="SE")
                    conv_pe_done = conv_last
                    pr.wait("scalar", "PE", conv_pe_done)
            pr.barrier()
        with contextlib.ExitStack() as psctx2:
            with contextlib.ExitStack() as ps2:
                d_ps = ps2.enter_context(nc.psum_tensor("p1_dps", [1, 512], F32))
                b_ps = ps2.enter_context(nc.psum_tensor("p1_bps", [DS, 512], F32))
                c_ps = ps2.enter_context(nc.psum_tensor("p1_cps", [DS, 512], F32))
                for tc in range(4):
                    last = 0
                    for dd in range(NDD):
                        st8 = (dd == 0)
                        sp8 = (dd == NDD - 1)
                        pr.op("tensor", lambda e, dd=dd, tc=tc, st8=st8, sp8=sp8: e.matmul(
                            out=d_ps[:], lhsT=wxT[dd][:, 0:1],
                            rhs=xc16[dd][:, tc * 512:(tc + 1) * 512], start=st8, stop=sp8))
                        pr.op("tensor", lambda e, dd=dd, tc=tc, st8=st8, sp8=sp8: e.matmul(
                            out=b_ps[:], lhsT=wxT[dd][:, 1:17],
                            rhs=xc16[dd][:, tc * 512:(tc + 1) * 512], start=st8, stop=sp8))
                        last = pr.op("tensor", lambda e, dd=dd, tc=tc, st8=st8, sp8=sp8: e.matmul(
                            out=c_ps[:], lhsT=wxT[dd][:, 17:33],
                            rhs=xc16[dd][:, tc * 512:(tc + 1) * 512], start=st8, stop=sp8), sem="PE")
                    pr.wait("scalar", "PE", last)
                    pr.op("scalar", lambda e, tc=tc: e.copy(out=drow[0:1, tc * 512:(tc + 1) * 512], in_=d_ps[:]))
                    pr.op("scalar", lambda e, tc=tc: e.copy(out=brow[:, tc * 512:(tc + 1) * 512], in_=b_ps[:]))
                    s_cr = pr.op("scalar", lambda e, tc=tc: e.copy(out=crow[:, tc * 512:(tc + 1) * 512], in_=c_ps[:]), sem="SE")
                    pr.wait("tensor", "SE", s_cr)
            pr.barrier()
        p1_ctx.close()
        pr.op("scalar", lambda e: e.activation(out=trow[:], in_=drow[:], func=AF.Abs))
        pr.drain("scalar")
        pr.op("scalar", lambda e: e.activation(out=trow[:], in_=trow[:], func=AF.Exp, scale=CN1r))
        pr.drain("scalar")
        s_t = pr.op("scalar", lambda e: e.activation(out=trow[:], in_=trow[:], func=AF.Ln, bias=C1r), sem="SE")
        pr.wait("vector", "SE", s_t)
        pr.op("vector", lambda e: e.tensor_scalar_max(out=drow[:], in0=drow[:], scalar1=0.0))
        pr.drain("vector")
        v_d = pr.op("vector", lambda e: e.tensor_add(out=drow[:], in0=drow[:], in1=trow[:]), sem="VE")
        pr.wait("sync", "VE", v_d)
        v_dw = pr.op("sync", lambda e: e.dma_start(out=rows32[0:1, :], in_=drow[:]), sem="SQ", n=16)
        pr.wait("sync", "SQ", v_dw)
        v_db = pr.op("sync", lambda e: e.dma_start(
            out=dbc16[:], in_=rows32[0:1, :].partition_broadcast(DS)), sem="SQ", n=16)
        pr.wait("vector", "SQ", v_db)
        v_dbm = pr.op("vector", lambda e: e.tensor_mul(out=db16[:], in0=brow[:], in1=dbc16[:]), sem="VE")
        s_c16 = pr.op("scalar", lambda e: e.copy(out=c16r[:], in_=crow[:]), sem="SE")
        pr.wait("sync", "VE", v_dbm)
        pr.op("sync", lambda e: e.dma_start(out=rows16[0:DS, :], in_=db16[:]), sem="SQ", n=16)
        pr.wait("sync", "SE", s_c16)
        v_rows = pr.op("sync", lambda e: e.dma_start(out=rows16[DS:2 * DS, :], in_=c16r[:]), sem="SQ", n=16)

        dbg_dump("xc", xc16, pr, root)
        if dbg:
            o_sz = nc.dram_tensor("dbg_sz", [DI, T], BF16, kind="ExternalOutput").ap()
            pr.op("sync", lambda e: e.dma_start(out=o_sz, in_=sz_scr), sem="SQ", n=16)
        dbg_dump("drow", [drow], pr, root)
        dbg_dump("dBrow", [db16], pr, root)
        dbg_dump("Crow", [c16r], pr, root)
        pr.barrier()
        p1_rows.close()

        if upto <= 1:
            _final_wait(pr)
            pr.finalize()
            return nc, dbg_out

        p2_ctx = contextlib.ExitStack()
        sb2 = lambda name, shape, dt: p2_ctx.enter_context(nc.sbuf_tensor(name, shape, dt, side="right"))
        delta_bc = sb2("p2_deltabc", [128, T], F32)
        dAb = [sb2(f"p2_dA_{i}", [128, T], F32) for i in range(2)]
        dBub = [sb2(f"p2_dBu_{i}", [128, T], BF16) for i in range(2)]
        hb = [sb2(f"p2_h_{i}", [128, T], BF16) for i in range(2)]
        Chb = [sb2(f"p2_Ch_{i}", [128, T], BF16) for i in range(2)]
        dbbc = [sb2(f"p2_dbbc_{i}", [128, T], BF16) for i in range(2)]
        cbc = [sb2(f"p2_cbc_{i}", [128, T], BF16) for i in range(2)]
        szt = [sb2(f"p2_szt_{i}", [128, T], BF16) for i in range(2)]
        pc32 = [sb2(f"p2_pc32_{i}", [128, C], F32) for i in range(2)]
        pc16 = [sb2(f"p2_pc16_{i}", [128, C], BF16) for i in range(2)]
        cast_steps = []
        for ee in range(E):
            for kc in range(NKC):
                cast_steps.append((ap["w1"][ee][kc * 128:(kc + 1) * 128, :],
                                   w1b_scr[ee][kc * 128:(kc + 1) * 128, :], None))
        for ee in range(E):
            for ki in range(NKC):
                cast_steps.append((ap["w2"][ee][ki * 128:(ki + 1) * 128, :],
                                   w2b_scr[ee][ki * 128:(ki + 1) * 128, :], None))
        for pi, wname in enumerate(("qp", "kp", "vp")):
            for ee in range(E):
                cast_steps.append((ap[wname][ee].rearrange("(kc p) h -> p kc h", p=128),
                                   qkv_scr[pi, ee], "r"))
        pc_state = {"i": 0, "cast": [0, 0], "st": [0, 0]}

        def emit_cast_step():
            i = pc_state["i"]
            if i >= len(cast_steps):
                return
            pc_state["i"] = i + 1
            srcap, dstap, mode = cast_steps[i]
            b = i % 2
            pr.wait("sync", "SE", pc_state["cast"][b])
            if mode == "r":
                v = pr.op("sync", lambda e, s=srcap, b=b: e.dma_start(
                    out=pc32[b][:].rearrange("p (kc h) -> p kc h", h=128), in_=s), sem="SQ", n=16)
            else:
                v = pr.op("sync", lambda e, s=srcap, b=b: e.dma_start(out=pc32[b][:], in_=s), sem="SQ", n=16)
            pr.wait("scalar", "SQ", max(v, pc_state["st"][b]))
            pc_state["cast"][b] = pr.op("scalar", lambda e, b=b: e.copy(out=pc16[b][:], in_=pc32[b][:]), sem="SE")
            pr.wait("sync", "SE", pc_state["cast"][b])
            pc_state["st"][b] = pr.op("sync", lambda e, d=dstap, b=b: e.dma_start(out=d, in_=pc16[b][:]), sem="SQ", n=16)
        v_dbc = pr.op("sync", lambda e: e.dma_start(
            out=delta_bc[:], in_=rows32[0:1, :].partition_broadcast(128)), sem="SQ", n=16)
        with contextlib.ExitStack() as psc:
            y_ps = [psc.enter_context(nc.psum_tensor(f"p2_y_{i}", [128, 512], F32)) for i in range(8)]
            dA_free = [0, 0]
            ch_free = [0, 0]
            bc_used = [0, 0]
            bc_used_po = [0, 0]
            po_dbu = [0, 0]
            bc_ld = [0, 0]
            sz_free = 0
            sz_ld = 0
            yp_done = 0
            it = 0
            for ddp in range(NDD // 2):
                dd0 = 2 * ddp
                pr.wait("sync", "VE", sz_free)
                for j in range(2):
                    sz_ld = pr.op("sync", lambda e, j=j, dd0=dd0: e.dma_start(
                        out=szt[j][:], in_=sz_scr[(dd0 + j) * 128:(dd0 + j + 1) * 128, :]), sem="SQ", n=16)
                def issue_bcast(s):
                    jb = s % 2
                    pr.wait("sync", "VE", bc_used[jb])
                    pr.op("sync", lambda e, s=s, jb=jb: e.dma_start(
                        out=dbbc[jb][:], in_=rows16[s:s + 1, :].partition_broadcast(128)), sem="SQ", n=16)
                    bc_ld[jb] = pr.op("sync", lambda e, s=s, jb=jb: e.dma_start(
                        out=cbc[jb][:], in_=rows16[DS + s:DS + s + 1, :].partition_broadcast(128)), sem="SQ", n=16)
                issue_bcast(0)
                for s in range(DS):
                    jb = s % 2
                    emit_cast_step()
                    if s + 1 < DS:
                        issue_bcast(s + 1)
                    for q in range(2):
                        dd = dd0 + q
                        ib = it % 2
                        pr.wait("scalar", "VE", dA_free[ib])
                        pr.wait("scalar", "SQ", v_dbc)
                        s_dA = pr.op("scalar", lambda e, ib=ib, dd=dd, s=s: e.activation(
                            out=dAb[ib][:], in_=delta_bc[:], func=AF.Exp,
                            scale=A_sb[:, dd * DS + s:dd * DS + s + 1]), sem="SE")
                        pr.wait("vector", "SQ", bc_ld[jb])
                        pr.wait("vector", "SE", s_dA)
                        pr.op("vector", lambda e, ib=ib, dd=dd, jb=jb: e.tensor_mul(
                            out=dBub[ib][:], in0=xc16[dd][:], in1=dbbc[jb][:]))
                        dA_free[ib] = pr.op("vector", lambda e, ib=ib: e.tensor_tensor_scan(
                            out=hb[ib][:], data0=dAb[ib][:], data1=dBub[ib][:], initial=0.0,
                            op0=ALU.mult, op1=ALU.add), sem="VE")
                        pr.wait("vector", "PE", ch_free[ib])
                        v_ch = pr.op("vector", lambda e, ib=ib, jb=jb: e.tensor_mul(
                            out=Chb[ib][:], in0=hb[ib][:], in1=cbc[jb][:]), sem="VE")
                        if q == 1:
                            bc_used[jb] = v_ch
                        pr.wait("tensor", "VE", v_ch)
                        if s == 0:
                            pr.wait("tensor", "VE", yp_done)
                        last = 0
                        for tc in range(4):
                            last = pr.op("tensor", lambda e, q=q, tc=tc, ib=ib, s=s: e.matmul(
                                out=y_ps[q * 4 + tc][:], lhsT=ident16[:],
                                rhs=Chb[ib][:, tc * 512:(tc + 1) * 512],
                                start=(s == 0), stop=(s == DS - 1)),
                                sem="PE" if tc == 3 else None)
                        ch_free[ib] = last
                        it += 1
                pr.wait("vector", "PE", ch_free[0])
                pr.wait("vector", "PE", ch_free[1])
                pr.wait("vector", "SQ", sz_ld)
                for q in range(2):
                    for tc in range(4):
                        yp_done = pr.op("vector", lambda e, q=q, tc=tc, dd0=dd0: e.tensor_mul(
                            out=xc16[dd0 + q][:, tc * 512:(tc + 1) * 512],
                            in0=y_ps[q * 4 + tc][:], in1=szt[q][:, tc * 512:(tc + 1) * 512]), sem="VE")
                sz_free = yp_done
            while pc_state["i"] < len(cast_steps):
                emit_cast_step()
            pr.barrier()
        p2_ctx.close()

        dbg_dump("yp", xc16, pr, root)
        if upto <= 2:
            _final_wait(pr)
            pr.finalize()
            return nc, dbg_out

        sout16 = [sb(f"sout_{m}", [128, T], BF16) for m in range(NKC)]
        p3_ctx = contextlib.ExitStack()
        sb3 = lambda name, shape, dt: p3_ctx.enter_context(nc.sbuf_tensor(name, shape, dt, side="right"))
        woT = [sb3(f"p3_woT_{d}", [128, C], BF16) for d in range(NDD)]
        wo32 = [sb3(f"p3_wo32_{i}", [128, DI], F32) for i in range(2)]
        wo16 = [sb3(f"p3_wo16_{i}", [128, DI], BF16) for i in range(2)]
        with contextlib.ExitStack() as psc:
            wtr_ps2 = [psc.enter_context(nc.psum_tensor(f"p3_wtr{i}", [128, 128], BF16)) for i in range(2)]
            wtr_ev = [0, 0]
            so_ps = [psc.enter_context(nc.psum_tensor(f"p3_so_{i}", [128, 512], F32)) for i in range(2)]
            cast_done = [0, 0]
            for mt in range(NKC):
                buf = mt % 2
                pr.wait("sync", "SE", cast_done[buf])
                v_ld = pr.op("sync", lambda e, b=buf, mt=mt: e.dma_start(
                    out=wo32[b][:], in_=ap["w_out"][mt * 128:(mt + 1) * 128, :]), sem="SQ", n=16)
                pr.wait("scalar", "SQ", v_ld)
                cast_done[buf] = pr.op("scalar", lambda e, b=buf: e.copy(
                    out=wo16[b][:], in_=wo32[b][:]), sem="SE")
                pr.wait("tensor", "SE", cast_done[buf])
                for dd in range(NDD):
                    pb = dd % 2
                    pr.wait("tensor", "SE", wtr_ev[pb])
                    pe_v = pr.op("tensor", lambda e, b=buf, dd=dd, pb=pb: e.transpose(
                        out=wtr_ps2[pb][:], in_=wo16[b][:, dd * 128:(dd + 1) * 128], identity=ident16[:]), sem="PE")
                    pr.wait("scalar", "PE", pe_v)
                    wtr_ev[pb] = pr.op("scalar", lambda e, dd=dd, mt=mt, pb=pb: e.copy(
                        out=woT[dd][:, mt * 128:(mt + 1) * 128], in_=wtr_ps2[pb][:]), sem="SE")
            pr.barrier()
            so_ev = [0, 0]
            for tc in range(4):
                for mt in range(NKC):
                    pb = mt % 2
                    pr.wait("tensor", "SE", so_ev[pb])
                    last = 0
                    for dd in range(NDD):
                        last = pr.op("tensor", lambda e, dd=dd, mt=mt, tc=tc, pb=pb: e.matmul(
                            out=so_ps[pb][:], lhsT=woT[dd][:, mt * 128:(mt + 1) * 128],
                            rhs=xc16[dd][:, tc * 512:(tc + 1) * 512],
                            start=(dd == 0), stop=(dd == NDD - 1)),
                            sem="PE" if dd == NDD - 1 else None)
                    pr.wait("scalar", "PE", last)
                    so_ev[pb] = pr.op("scalar", lambda e, mt=mt, tc=tc, pb=pb: e.copy(
                        out=sout16[mt][:, tc * 512:(tc + 1) * 512], in_=so_ps[pb][:]), sem="SE")
            pr.barrier()
        p3_ctx.close()

        dbg_dump("sout", sout16, pr, root)
        if upto <= 3:
            _final_wait(pr)
            pr.finalize()
            return nc, dbg_out

        hs16 = xc16[0:NKC]
        n1T = xc16[NKC:2 * NKC]

        for kc in range(NKC):
            pr.wait("vector", "SQ", v_c2b)
            pr.op("vector", lambda e, kc=kc: e.tensor_mul(
                out=n1T[kc][:], in0=xnT[kc][:], in1=c2bc16[:]))
            pr.drain("vector")
            pr.op("vector", lambda e, kc=kc: e.tensor_scalar(
                out=n1T[kc][:], in0=n1T[kc][:], scalar1=_col(ln1w, kc), scalar2=_col(ln1b, kc),
                op0=ALU.mult, op1=ALU.add))
        v_n1 = pr.fence("vector", "VE")

        def gating(sim_ap, gates_ap, nsrc, rw16out, sbg, tag, v_ready, fp32=False, affine=None):
            sim32 = sbg(tag + "_sim32", [128, E * NKC], F32)
            nsim16 = sbg(tag + "_nsim16", [128, E * NKC], BF16)
            if fp32:
                xst32 = sbg(tag + "_xst32", [128, C], F32)
                xnT32c = [sbg(tag + f"_xnT32_{k}", [128, 512], F32) for k in range(NKC)]
                c2b32 = sbg(tag + "_c2b32", [128, 512], F32)
                nsq32 = sbg(tag + "_nsq32", [128, 512], F32)
                c2flat = c2scr.rearrange("a b -> (a b)").unsqueeze(0)
                aw, ab = affine
            rnb8 = sbg(tag + "_rnb8", [128, E], F32)
            sgr = sbg(tag + "_sgr", [1, E], F32)
            nrm = sbg(tag + "_nrm", [1, E], F32)
            colsq = sbg(tag + "_colsq", [1, E * NKC], F32)
            nsq16 = sbg(tag + "_nsq", [128, T], BF16)
            stub8 = sbg(tag + "_stub8", [8, 512], F32)
            stub1 = sbg(tag + "_stub1", [1, 512], F32)
            G = sbg(tag + "_G", [128, 128], F32)
            Gs = [sbg(tag + f"_Gs{i}", [128, 128], F32) for i in range(3)]
            Gm = [sbg(tag + f"_Gm{i}", [128, 16], F32) for i in range(2)]
            rnG = sbg(tag + "_rnG", [128, 16], F32)
            sgbig = sbg(tag + "_sgbig", [128, 128], F32)
            gz = sbg(tag + "_gz", [128, 128], F32)
            g3 = lambda t: t[:].rearrange("p (a b) -> p a b", b=E)
            gm3 = lambda t: t[:].unsqueeze(2).broadcast_to([128, 16, E])
            pr.op("vector", lambda e: e.memset(gz[:], 0.0))
            v_ld = 0
            for kc in range(NKC):
                v_ld = pr.op("sync", lambda e, kc=kc: e.dma_start(
                    out=sim32[:, kc * E:(kc + 1) * E], in_=sim_ap[kc * 128:(kc + 1) * 128, :]), sem="SQ", n=16)
            v_g = pr.op("sync", lambda e: e.dma_start(out=sgr[:], in_=gates_ap), sem="SQ", n=16)
            pr.wait("scalar", "SQ", v_g)
            pr.op("scalar", lambda e: e.activation(out=sgr[:], in_=sgr[:], func=AF.Sigmoid))
            s_sg = pr.fence("scalar", "SE")
            pr.wait("sync", "SE", s_sg)
            v_w = pr.op("sync", lambda e: e.dma_start(out=rows32[1:2, 0:E], in_=sgr[:]), sem="SQ", n=16)
            pr.wait("sync", "SQ", v_w)
            for j in range(16):
                pr.op("sync", lambda e, j=j: e.dma_start(
                    out=sgbig[:, j * E:(j + 1) * E],
                    in_=rows32[1:2, 0:E].partition_broadcast(128)), sem="SQ", n=16)
            v_sgb = pr.cnt["SQ"]
            with contextlib.ExitStack() as psg:
                cs_ps = psg.enter_context(nc.psum_tensor(tag + "_csps", [1, E * NKC], F32))
                nq_ps = [psg.enter_context(nc.psum_tensor(tag + f"_nq{i}", [1, 512], F32)) for i in range(1 if fp32 else 4)]
                raw_ps = psg.enter_context(nc.psum_tensor(tag + "_rawps", [8, 512], F32))
                gt_ps = psg.enter_context(nc.psum_tensor(tag + "_gtps", [128, 8], F32))
                gb_ps = psg.enter_context(nc.psum_tensor(tag + "_gbps", [8, 128], F32))
                pr.wait("vector", "SQ", v_ld)
                pr.op("vector", lambda e: e.tensor_mul(out=Gs[0][:, 0:E * NKC], in0=sim32[:], in1=sim32[:]))
                v_sq = pr.fence("vector", "VE")
                pr.wait("tensor", "VE", v_sq)
                pe_cs = pr.op("tensor", lambda e: e.matmul(
                    out=cs_ps[:], lhsT=ones32[:, 0:1], rhs=Gs[0][:, 0:E * NKC],
                    start=True, stop=True), sem="PE")
                pr.wait("scalar", "PE", pe_cs)
                s_cs = pr.op("scalar", lambda e: e.copy(out=colsq[:], in_=cs_ps[:]), sem="SE")
                pr.wait("vector", "SE", s_cs)
                pr.op("vector", lambda e: e.tensor_reduce(
                    out=nrm[:].unsqueeze(2),
                    in_=colsq[:].rearrange("p (kc e) -> p e kc", e=E), axis=AX.X, op=ALU.add))
                v_nr = pr.fence("vector", "VE")
                pr.wait("scalar", "VE", v_nr)
                pr.op("scalar", lambda e: e.activation(out=nrm[:], in_=nrm[:], func=AF.Ln))
                pr.drain("scalar")
                pr.op("scalar", lambda e: e.activation(out=nrm[:], in_=nrm[:], func=AF.Exp, scale=CNHr))
                s_nr = pr.fence("scalar", "SE")
                pr.wait("sync", "SE", s_nr)
                v_nw = pr.op("sync", lambda e: e.dma_start(out=rows32[2:3, 0:E], in_=nrm[:]), sem="SQ", n=16)
                pr.wait("sync", "SQ", v_nw)
                v_nb = pr.op("sync", lambda e: e.dma_start(
                    out=rnb8[:], in_=rows32[2:3, 0:E].partition_broadcast(128)), sem="SQ", n=16)
                pr.wait("vector", "SQ", v_nb)
                for kc in range(NKC):
                    pr.op("vector", lambda e, kc=kc: e.tensor_mul(
                        out=nsim16[:, kc * E:(kc + 1) * E], in0=sim32[:, kc * E:(kc + 1) * E], in1=rnb8[:]))
                if fp32:
                    pr.drain("vector")
                    for kc in range(NKC):
                        pr.op("vector", lambda e, kc=kc: e.tensor_mul(
                            out=sim32[:, kc * E:(kc + 1) * E], in0=sim32[:, kc * E:(kc + 1) * E], in1=rnb8[:]))
                v_ns = pr.fence("vector", "VE")
                if fp32:
                    tr32_ps2 = [psg.enter_context(nc.psum_tensor(tag + f"_tr32{i}", [128, 128], F32)) for i in range(2)]
                    tr32_ev = [0, 0]
                    pr.wait("tensor", "VE", v_ns)
                    v_rw_prev = 0
                    ev_prev = 0
                    v_aff_prev = 0
                    for tc in range(4):
                        x_dma = 0
                        for ttl in range(4):
                            tt = tc * 4 + ttl
                            pr.wait("sync", "PE", ev_prev)
                            x_dma = pr.op("sync", lambda e, tt=tt: e.dma_start(
                                out=xst32[:], in_=xn32_scr[tt * 128:(tt + 1) * 128, :]), sem="SQ", n=16)
                            pr.wait("tensor", "SQ", x_dma)
                            for kc in range(NKC):
                                pb = kc % 2
                                pr.wait("tensor", "SE", tr32_ev[pb])
                                pe_t = pr.op("tensor", lambda e, kc=kc, pb=pb: e.transpose(
                                    out=tr32_ps2[pb][:], in_=xst32[:, kc * 128:(kc + 1) * 128],
                                    identity=ident32[:]), sem="PE")
                                pr.wait("scalar", "PE", pe_t)
                                tr32_ev[pb] = pr.op("scalar", lambda e, kc=kc, ttl=ttl, pb=pb: e.copy(
                                    out=xnT32c[kc][:, ttl * 128:(ttl + 1) * 128], in_=tr32_ps2[pb][:]), sem="SE")
                            ev_prev = pr.cnt["PE"]
                        ev_all = pr.cnt["SE"]
                        pr.wait("sync", "VE", v_aff_prev)
                        v_c2c = pr.op("sync", lambda e, tc=tc: e.dma_start(
                            out=c2b32[:], in_=c2flat[0:1, tc * 512:(tc + 1) * 512].partition_broadcast(128)), sem="SQ", n=16)
                        pr.wait("vector", "SQ", v_c2c)
                        pr.wait("vector", "SE", ev_all)
                        for kc in range(NKC):
                            pr.op("vector", lambda e, kc=kc: e.tensor_mul(
                                out=xnT32c[kc][:], in0=xnT32c[kc][:], in1=c2b32[:]))
                            pr.drain("vector")
                            pr.op("vector", lambda e, kc=kc: e.tensor_scalar(
                                out=xnT32c[kc][:], in0=xnT32c[kc][:], scalar1=_col(aw, kc), scalar2=_col(ab, kc),
                                op0=ALU.mult, op1=ALU.add))
                        v_aff = pr.fence("vector", "VE")
                        v_aff_prev = v_aff
                        pr.wait("scalar", "VE", v_aff)
                        nq_last = 0
                        for kc in range(NKC):
                            s_sq = pr.op("scalar", lambda e, kc=kc: e.activation(
                                out=nsq32[:], in_=xnT32c[kc][:], func=AF.Square), sem="SE")
                            pr.wait("tensor", "SE", s_sq)
                            nq_last = pr.op("tensor", lambda e, kc=kc: e.matmul(
                                out=nq_ps[0][:], lhsT=ones32[:, 0:1], rhs=nsq32[:],
                                start=(kc == 0), stop=(kc == NKC - 1)), sem="PE")
                            pr.wait("scalar", "PE", nq_last)
                        pr.wait("scalar", "SQ", v_rw_prev)
                        pr.op("scalar", lambda e: e.copy(out=stub1[:], in_=nq_ps[0][:]))
                        pr.drain("scalar")
                        pr.op("scalar", lambda e: e.activation(out=stub1[:], in_=stub1[:], func=AF.Ln))
                        pr.drain("scalar")
                        pr.op("scalar", lambda e: e.activation(out=stub1[:], in_=stub1[:], func=AF.Exp, scale=CNHr))
                        s_rn = pr.fence("scalar", "SE")
                        pr.wait("sync", "SE", s_rn)
                        v_rw_prev = pr.op("sync", lambda e, tc=tc: e.dma_start(
                            out=rows32[3:4, tc * 512:(tc + 1) * 512], in_=stub1[:]), sem="SQ", n=16)
                        pr.wait("tensor", "SE", s_rn)
                        raw_last = 0
                        for kc in range(NKC):
                            raw_last = pr.op("tensor", lambda e, kc=kc, tc=tc: e.matmul(
                                out=raw_ps[:], lhsT=sim32[:, kc * E:(kc + 1) * E],
                                rhs=xnT32c[kc][:], start=(kc == 0), stop=(kc == NKC - 1)),
                                sem="PE" if kc == NKC - 1 else None)
                        pr.wait("scalar", "PE", raw_last)
                        s_st = pr.op("scalar", lambda e: e.copy(out=stub8[:], in_=raw_ps[:]), sem="SE")
                        pr.wait("tensor", "SE", s_st)
                        for j in range(4):
                            pe_g = pr.op("tensor", lambda e, j=j: e.transpose(
                                out=gt_ps[:], in_=stub8[:, j * 128:(j + 1) * 128],
                                identity=ident32[0:8, 0:8]), sem="PE")
                            pr.wait("scalar", "PE", pe_g)
                            s_gv = pr.op("scalar", lambda e, tc=tc, j=j: e.copy(
                                out=G[:, (tc * 4 + j) * E:(tc * 4 + j + 1) * E], in_=gt_ps[:]), sem="SE")
                            pr.wait("tensor", "SE", s_gv)
                        ev_prev = pr.cnt["PE"]
                        pr.barrier()
                    pr.wait("sync", "SQ", pr.cnt["SQ"])
                    v_rng = pr.op("sync", lambda e: e.dma_start(
                        out=rnG[:], in_=rows32[3:4, :].rearrange("one (tt p) -> (one p) tt", p=128),
                        allow_slow_non_contiguous=True), sem="SQ", n=16)
                if not fp32:
                    pr.wait("scalar", "VE", v_ready)
                    nq_last = 0
                    for kc in range(NKC):
                        s_sq = pr.op("scalar", lambda e, kc=kc: e.activation(
                            out=nsq16[:], in_=nsrc[kc][:], func=AF.Square), sem="SE")
                        pr.wait("tensor", "SE", s_sq)
                        for tc in range(4):
                            nq_last = pr.op("tensor", lambda e, kc=kc, tc=tc: e.matmul(
                                out=nq_ps[tc][:], lhsT=ones16[:, 0:1],
                                rhs=nsq16[:, tc * 512:(tc + 1) * 512],
                                start=(kc == 0), stop=(kc == NKC - 1)),
                                sem="PE" if (kc == NKC - 1) else None)
                        if kc < NKC - 1:
                            nq_mark = pr.op("tensor", lambda e: e.nop(), sem="PE")
                            pr.wait("scalar", "PE", nq_mark)
                    v_rw_prev = 0
                    for tc in range(4):
                        pr.wait("scalar", "PE", nq_last)
                        pr.wait("scalar", "SQ", v_rw_prev)
                        pr.op("scalar", lambda e, tc=tc: e.copy(out=stub1[:], in_=nq_ps[tc][:]))
                        pr.drain("scalar")
                        pr.op("scalar", lambda e: e.activation(out=stub1[:], in_=stub1[:], func=AF.Ln))
                        pr.drain("scalar")
                        pr.op("scalar", lambda e: e.activation(out=stub1[:], in_=stub1[:], func=AF.Exp, scale=CNHr))
                        s_rn = pr.fence("scalar", "SE")
                        pr.wait("sync", "SE", s_rn)
                        v_rw_prev = pr.op("sync", lambda e, tc=tc: e.dma_start(
                            out=rows32[3:4, tc * 512:(tc + 1) * 512], in_=stub1[:]), sem="SQ", n=16)
                    pr.barrier()
                    pr.wait("sync", "SQ", pr.cnt["SQ"])
                    v_rng = pr.op("sync", lambda e: e.dma_start(
                        out=rnG[:], in_=rows32[3:4, :].rearrange("one (tt p) -> (one p) tt", p=128),
                        allow_slow_non_contiguous=True), sem="SQ", n=16)
                    pr.wait("tensor", "VE", v_ns)
                    for tc in range(4):
                        raw_last = 0
                        for kc in range(NKC):
                            raw_last = pr.op("tensor", lambda e, kc=kc, tc=tc: e.matmul(
                                out=raw_ps[:], lhsT=nsim16[:, kc * E:(kc + 1) * E],
                                rhs=nsrc[kc][:, tc * 512:(tc + 1) * 512],
                                start=(kc == 0), stop=(kc == NKC - 1)),
                                sem="PE" if kc == NKC - 1 else None)
                        pr.wait("scalar", "PE", raw_last)
                        s_st = pr.op("scalar", lambda e: e.copy(out=stub8[:], in_=raw_ps[:]), sem="SE")
                        pr.wait("tensor", "SE", s_st)
                        for j in range(4):
                            pe_g = pr.op("tensor", lambda e, j=j: e.transpose(
                                out=gt_ps[:], in_=stub8[:, j * 128:(j + 1) * 128],
                                identity=ident32[0:8, 0:8]), sem="PE")
                            pr.wait("scalar", "PE", pe_g)
                            s_gv = pr.op("scalar", lambda e, tc=tc, j=j: e.copy(
                                out=G[:, (tc * 4 + j) * E:(tc * 4 + j + 1) * E], in_=gt_ps[:]), sem="SE")
                            pr.wait("tensor", "SE", s_gv)

                pr.barrier()
                v_sgready = v_sgb
                pr.wait("vector", "SE", pr.cnt["SE"])
                pr.wait("vector", "SQ", max(v_rng, v_sgready))
                D = lambda: pr.drain("vector")
                pr.op("vector", lambda e: e.tensor_mul(out=g3(G), in0=g3(G), in1=gm3(rnG))); D()
                pr.op("vector", lambda e: e.tensor_sub(out=G[:], in0=G[:], in1=sgbig[:])); D()
                if dbg:
                    o_g = nc.dram_tensor("dbg_logits" + tag, [128, 128], F32, kind="ExternalOutput").ap()
                    v_gd = pr.fence("vector", "VE")
                    pr.wait("sync", "VE", v_gd)
                    pr.op("sync", lambda e, o_g=o_g: e.dma_start(out=o_g, in_=G[:]), sem="SQ", n=16)
                    o_rn = nc.dram_tensor("dbg_rng" + tag, [128, 16], F32, kind="ExternalOutput").ap()
                    pr.op("sync", lambda e, o_rn=o_rn: e.dma_start(out=o_rn, in_=rnG[:]), sem="SQ", n=16)
                    o_sg = nc.dram_tensor("dbg_sgb" + tag, [128, 128], F32, kind="ExternalOutput").ap()
                    pr.op("sync", lambda e, o_sg=o_sg: e.dma_start(out=o_sg, in_=sgbig[:]), sem="SQ", n=16)
                pr.op("vector", lambda e: e.tensor_reduce(out=Gm[0][:].unsqueeze(2), in_=g3(G), axis=AX.X, op=ALU.max)); D()
                pr.op("vector", lambda e: e.tensor_tensor(out=g3(Gs[0]), in0=g3(G), in1=gm3(Gm[0]), op=ALU.is_equal)); D()
                pr.op("vector", lambda e: e.scalar_tensor_tensor(out=Gs[1][:], in0=Gs[0][:], scalar=NEGBIG, in1=G[:], op0=ALU.mult, op1=ALU.add)); D()
                pr.op("vector", lambda e: e.tensor_reduce(out=Gm[1][:].unsqueeze(2), in_=g3(Gs[1]), axis=AX.X, op=ALU.max)); D()
                pr.op("vector", lambda e: e.tensor_tensor(out=g3(Gs[1]), in0=g3(Gs[1]), in1=gm3(Gm[1]), op=ALU.is_equal)); D()
                pr.op("vector", lambda e: e.tensor_add(out=Gs[0][:], in0=Gs[0][:], in1=Gs[1][:])); D()
                pr.op("vector", lambda e: e.tensor_tensor(out=Gs[1][:], in0=G[:], in1=gz[:], op=ALU.is_gt)); D()
                pr.op("vector", lambda e: e.tensor_reduce(out=Gm[0][:].unsqueeze(2), in_=g3(Gs[1]), axis=AX.X, op=ALU.add)); D()
                pr.op("vector", lambda e: e.tensor_tensor(out=Gm[1][:], in0=Gm[0][:], in1=gz[:, 0:16], op=ALU.is_equal)); D()
                pr.op("vector", lambda e: e.tensor_tensor(out=g3(Gs[0]), in0=g3(Gs[0]), in1=gm3(Gm[1]), op=ALU.mult)); D()
                pr.op("vector", lambda e: e.tensor_add(out=Gs[1][:], in0=Gs[1][:], in1=Gs[0][:])); D()
                pr.op("vector", lambda e: e.tensor_scalar_max(out=Gs[2][:], in0=G[:], scalar1=0.0)); D()
                pr.op("vector", lambda e: e.tensor_mul(out=Gs[2][:], in0=Gs[2][:], in1=Gs[1][:])); D()
                pr.op("vector", lambda e: e.scalar_tensor_tensor(out=Gs[2][:], in0=Gs[1][:], scalar=-NEGBIG, in1=Gs[2][:], op0=ALU.mult, op1=ALU.add)); D()
                pr.op("vector", lambda e: e.tensor_scalar_add(out=Gs[2][:], in0=Gs[2][:], scalar1=NEGBIG))
                v_mk = pr.fence("vector", "VE")
                pr.wait("scalar", "VE", v_mk)
                pr.op("scalar", lambda e: e.activation(out=Gs[2][:], in_=Gs[2][:], func=AF.Exp))
                s_ex = pr.fence("scalar", "SE")
                pr.wait("vector", "SE", s_ex)
                pr.op("vector", lambda e: e.tensor_reduce(out=Gm[0][:].unsqueeze(2), in_=g3(Gs[2]), axis=AX.X, op=ALU.add)); D()
                pr.op("vector", lambda e: e.reciprocal(out=Gm[1][:], in_=Gm[0][:])); D()
                pr.op("vector", lambda e: e.tensor_mul(out=g3(G), in0=g3(Gs[2]), in1=gm3(Gm[1])))
                v_rwg = pr.fence("vector", "VE")
                pr.wait("tensor", "VE", v_rwg)
                for j in range(16):
                    pe_b = pr.op("tensor", lambda e, j=j: e.transpose(
                        out=gb_ps[:], in_=G[:, j * E:(j + 1) * E], identity=ident32[:]), sem="PE")
                    pr.wait("scalar", "PE", pe_b)
                    s_rv = pr.op("scalar", lambda e, j=j: e.copy(
                        out=rw16out[0:8, j * 128:(j + 1) * 128], in_=gb_ps[:]), sem="SE")
                    pr.wait("tensor", "SE", s_rv)
                return pr.fence("scalar", "SE")

        p4_ctx = contextlib.ExitStack()
        sb4 = lambda nm, shape, dt: p4_ctx.enter_context(nc.sbuf_tensor(nm, shape, dt, side="right"))
        rwA = sb4("p4_rwA", [8, T], BF16)
        g1_scope = contextlib.ExitStack()
        sbg1 = lambda nm, shape, dt: g1_scope.enter_context(nc.sbuf_tensor(nm, shape, dt, side="right"))
        s_rw = gating(ap["sim1"], ap["gates1"], n1T, rwA, sbg1, "g1", v_n1,
                      fp32=True, affine=(ln1w, ln1b))
        pr.barrier()
        g1_scope.close()

        q16 = sb4("p4_q16", [128, T], BF16)
        k16 = sb4("p4_k16", [128, T], BF16)
        v16 = sb4("p4_v16", [128, T], BF16)
        vT16 = [sb4(f"p4_vT_{i}", [128, 128], BF16) for i in range(NTT)]
        probs16 = sb4("p4_probs", [128, T], BF16)
        n1e = [sb4(f"p4_n1e_{i}", [128, 512], BF16) for i in range(2)]
        rwbc = sb4("p4_rwbc", [128, 512], BF16)
        rwbcq = [sb4(f"p4_rwbcq_{i}", [128, 128], BF16) for i in range(E)]
        ae16 = [sb4(f"p4_ae_{i}", [128, 128], BF16) for i in range(E)]
        exT = [sb4(f"p4_exT_{i}", [128, 128], BF16) for i in range(2)]
        wst32 = sb4("p4_wst32", [128, C], F32)
        wst16 = [sb4(f"p4_wq_{i}", [128, C], BF16) for i in range(3)]
        ow16 = [sb4(f"p4_ow_{i}", [128, C], BF16) for i in range(E)]
        mxt = sb4("p4_mx", [128, 8], F32)
        with contextlib.ExitStack() as ps4:
            qkv_ps = [ps4.enter_context(nc.psum_tensor(f"p4_qkv_{i}", [128, 512], F32)) for i in range(3)]
            rb_ps = ps4.enter_context(nc.psum_tensor("p4_rb", [128, 512], F32))
            v_ow = 0
            s_owc = 0
            for ee in range(E):
                pr.wait("sync", "SE", s_owc)
                v_ow = pr.op("sync", lambda e, ee=ee: e.dma_start(
                    out=wst32[:], in_=ap["op"][ee]), sem="SQ", n=16)
                pr.wait("scalar", "SQ", v_ow)
                s_owc = pr.op("scalar", lambda e, ee=ee: e.copy(out=ow16[ee][:], in_=wst32[:]), sem="SE")
            qkv_ev = 0
            pe_prev_e = 0
            pe_n1e = [0, 0]
            rb_used = 0
            for tc in range(4):
                pr.wait("tensor", "SE", s_rw)
                pr.wait("tensor", "SE", qkv_ev)
                pe_done = 0
                for ee in range(E):
                    pr.wait("tensor", "VE", rb_used)
                    pe_rb = pr.op("tensor", lambda e, ee=ee, tc=tc: e.matmul(
                        out=rb_ps[:], lhsT=sel8[:, ee * 128:(ee + 1) * 128],
                        rhs=rwA[0:8, tc * 512:(tc + 1) * 512], start=True, stop=True), sem="PE")
                    pr.wait("scalar", "PE", pe_rb)
                    s_rb = pr.op("scalar", lambda e: e.copy(out=rwbc[:], in_=rb_ps[:]), sem="SE")
                    s_w = 0
                    pr.wait("sync", "PE", pe_prev_e)
                    for pi in range(3):
                        s_w = pr.op("sync", lambda e, ee=ee, pi=pi: e.dma_start(
                            out=wst16[pi][:], in_=qkv_scr[pi, ee]), sem="SQ", n=16)
                    last3 = 0
                    for kc in range(NKC):
                        ib = kc % 2
                        pr.wait("vector", "SE", s_rb)
                        pr.wait("vector", "PE", pe_n1e[ib])
                        v_ne = pr.op("vector", lambda e, kc=kc, tc=tc, ib=ib: e.tensor_mul(
                            out=n1e[ib][:], in0=n1T[kc][:, tc * 512:(tc + 1) * 512], in1=rwbc[:]), sem="VE")
                        pr.wait("tensor", "VE", v_ne)
                        pr.wait("tensor", "SQ", s_w)
                        for pi in range(3):
                            last3 = pr.op("tensor", lambda e, pi=pi, kc=kc, ee=ee, ib=ib: e.matmul(
                                out=qkv_ps[pi][:], lhsT=wst16[pi][:, kc * 128:(kc + 1) * 128],
                                rhs=n1e[ib][:],
                                start=(ee == 0 and kc == 0), stop=(ee == E - 1 and kc == NKC - 1)),
                                sem="PE" if pi == 2 else None)
                        pe_n1e[ib] = last3
                    pe_prev_e = last3
                    rb_used = pr.cnt["VE"]
                pe_done = pe_prev_e
                pr.wait("scalar", "PE", pe_done)
                pr.op("scalar", lambda e, tc=tc: e.activation(
                    out=q16[:, tc * 512:(tc + 1) * 512], in_=qkv_ps[0][:], func=AF.Copy, scale=float(RS)))
                pr.op("scalar", lambda e, tc=tc: e.copy(out=k16[:, tc * 512:(tc + 1) * 512], in_=qkv_ps[1][:]))
                qkv_ev = pr.op("scalar", lambda e, tc=tc: e.copy(
                    out=v16[:, tc * 512:(tc + 1) * 512], in_=qkv_ps[2][:]), sem="SE")
            pr.barrier()
        with contextlib.ExitStack() as ps5:
            sc_ps = [ps5.enter_context(nc.psum_tensor(f"p4_sc_{i}", [128, 512], F32)) for i in range(4)]
            str_ps = ps5.enter_context(nc.psum_tensor("p4_str", [128, 128], BF16))
            av_ps = ps5.enter_context(nc.psum_tensor("p4_av", [128, 128], F32))
            hs_ps = ps5.enter_context(nc.psum_tensor("p4_hs", [128, 1024], F32))
            for kb in range(NTT):
                pe_v = pr.op("tensor", lambda e, kb=kb: e.transpose(
                    out=str_ps[:], in_=v16[:, kb * 128:(kb + 1) * 128], identity=ident16[:]), sem="PE")
                pr.wait("scalar", "PE", pe_v)
                s_ev = pr.op("scalar", lambda e, kb=kb: e.copy(out=vT16[kb][:], in_=str_ps[:]), sem="SE")
                pr.wait("tensor", "SE", s_ev)
            hs_ev = 0
            for qb in range(NTT):
                W = (qb + 1) * 128
                nch = (W + 511) // 512
                if qb % 4 == 0:
                    pass
                sc_last = []
                for ci in range(nch):
                    c0 = ci * 512
                    c1 = min(W, c0 + 512)
                    v_s = pr.op("tensor", lambda e, qb=qb, ci=ci, c0=c0, c1=c1: e.matmul(
                        out=sc_ps[ci][:, 0:c1 - c0], lhsT=q16[:, qb * 128:(qb + 1) * 128],
                        rhs=k16[:, c0:c1], start=True, stop=True), sem="PE")
                    sc_last.append(v_s)
                off = qb * 128 - (nch - 1) * 512
                pr.wait("vector", "PE", sc_last[-1])
                pr.op("vector", lambda e, off=off, ci=nch - 1: e.tensor_add(
                    out=sc_ps[ci][:, off:off + 128], in0=sc_ps[ci][:, off:off + 128], in1=dmask[:]))
                pr.drain("vector")
                for ci in range(nch):
                    c0 = ci * 512
                    c1 = min(W, c0 + 512)
                    pr.op("vector", lambda e, ci=ci, c0=c0, c1=c1: e.reduce_max(
                        out=_col(mxt, ci), in_=sc_ps[ci][:, 0:c1 - c0], axis=AX.X))
                pr.drain("vector")
                for ci in range(1, nch):
                    pr.op("vector", lambda e, ci=ci: e.tensor_max(out=_col(mxt, 0), in0=_col(mxt, 0), in1=_col(mxt, ci)))
                    pr.drain("vector")
                pr.op("vector", lambda e: e.tensor_scalar_mul(out=_col(mxt, 4), in0=_col(mxt, 0), scalar1=-1.0))
                v_mx = pr.fence("vector", "VE")
                pr.wait("scalar", "VE", v_mx)
                for ci in range(nch):
                    c0 = ci * 512
                    c1 = min(W, c0 + 512)
                    pr.op("scalar", lambda e, ci=ci, c0=c0, c1=c1: e.activation(
                        out=probs16[:, c0:c1], in_=sc_ps[ci][:, 0:c1 - c0], func=AF.Exp,
                        bias=_col(mxt, 4), accum_out=_col(mxt, 8 + ci) if False else _col(mxt, ci)))
                s_ex = pr.fence("scalar", "SE")
                pr.wait("vector", "SE", s_ex)
                for ci in range(1, nch):
                    pr.op("vector", lambda e, ci=ci: e.tensor_add(out=_col(mxt, 0), in0=_col(mxt, 0), in1=_col(mxt, ci)))
                    pr.drain("vector")
                pr.op("vector", lambda e: e.reciprocal(out=_col(mxt, 5), in_=_col(mxt, 0)))
                pr.drain("vector")
                v_pr = pr.op("vector", lambda e, W=W: e.tensor_scalar_mul(
                    out=probs16[:, 0:W], in0=probs16[:, 0:W], scalar1=_col(mxt, 5)), sem="VE")
                pr.wait("tensor", "VE", v_pr)
                rbq_ev = [0, 0]
                for ee in range(E):
                    pbq = ee % 2
                    pr.wait("tensor", "SE", rbq_ev[pbq])
                    pe_rb = pr.op("tensor", lambda e, ee=ee, qb=qb, pbq=pbq: e.matmul(
                        out=sc_ps[pbq][:, 0:128], lhsT=sel8[:, ee * 128:(ee + 1) * 128],
                        rhs=rwA[0:8, qb * 128:(qb + 1) * 128], start=True, stop=True), sem="PE")
                    pr.wait("scalar", "PE", pe_rb)
                    rbq_ev[pbq] = pr.op("scalar", lambda e, ee=ee, pbq=pbq: e.copy(
                        out=rwbcq[ee][:], in_=sc_ps[pbq][:, 0:128]), sem="SE")
                pr.wait("tensor", "SE", rbq_ev[0])
                pr.wait("tensor", "SE", rbq_ev[1])
                av_last = 0
                for kb in range(qb + 1):
                    pe_t = pr.op("tensor", lambda e, kb=kb: e.transpose(
                        out=str_ps[:], in_=probs16[:, kb * 128:(kb + 1) * 128], identity=ident16[:]), sem="PE")
                    pr.wait("scalar", "PE", pe_t)
                    s_x = pr.op("scalar", lambda e, kb=kb: e.copy(out=exT[kb % 2][:], in_=str_ps[:]), sem="SE")
                    pr.wait("tensor", "SE", s_x)
                    av_last = pr.op("tensor", lambda e, kb=kb, qb=qb: e.matmul(
                        out=av_ps[:], lhsT=vT16[kb][:], rhs=exT[kb % 2][:],
                        start=(kb == 0), stop=(kb == qb)), sem="PE")
                pr.wait("vector", "PE", av_last)
                v_ae = 0
                for ee in range(E):
                    v_ae = pr.op("vector", lambda e, ee=ee: e.tensor_mul(
                        out=ae16[ee][:], in0=av_ps[:], in1=rwbcq[ee][:]), sem="VE")
                pr.wait("tensor", "VE", v_ae)
                pr.wait("tensor", "SE", hs_ev)
                for mc in range(NKC):
                    pr.op("tensor", lambda e, mc=mc, qb=qb: e.matmul(
                        out=hs_ps[:, mc * 128:(mc + 1) * 128], lhsT=ident16[:],
                        rhs=xnT[mc][:, qb * 128:(qb + 1) * 128],
                        start=(mc == 0 or mc == 4), stop=False))
                hs_last = 0
                for ee in range(E):
                    for mc in range(NKC):
                        hs_last = pr.op("tensor", lambda e, ee=ee, mc=mc: e.matmul(
                            out=hs_ps[:, mc * 128:(mc + 1) * 128], lhsT=ow16[ee][:, mc * 128:(mc + 1) * 128],
                            rhs=ae16[ee][:], start=False,
                            stop=(ee == E - 1 and mc in (3, 7))),
                            sem="PE" if (ee == E - 1 and mc == NKC - 1) else None)
                pr.wait("scalar", "PE", hs_last)
                for mc in range(NKC):
                    hs_ev = pr.op("scalar", lambda e, mc=mc, qb=qb: e.copy(
                        out=hs16[mc][:, qb * 128:(qb + 1) * 128],
                        in_=hs_ps[:, mc * 128:(mc + 1) * 128]), sem="SE")
            pr.barrier()
        p4_ctx.close()

        dbg_dump("hs", hs16, pr, root)
        dbg_dump("rwrows", [rwA], pr, root)
        dbg_dump("q16", [q16], pr, root)
        dbg_dump("k16", [k16], pr, root)
        dbg_dump("v16", [v16], pr, root)
        dbg_dump("n1T0", [n1T[0]], pr, root)
        if upto <= 4:
            _final_wait(pr)
            pr.finalize()
            return nc, dbg_out

        n2T = xc16[NKC:2 * NKC]
        rwB = root.enter_context(nc.sbuf_tensor("p5_rwB", [8, T], BF16, side="right"))
        p5_ctx = contextlib.ExitStack()
        sb5 = lambda nm, shape, dt: p5_ctx.enter_context(nc.sbuf_tensor(nm, shape, dt, side="right"))
        sq16b = sb5("p5_sq16", [128, T], BF16)
        stubA = sb5("p5_stubA", [1, 512], F32)
        stubB = sb5("p5_stubB", [1, 512], F32)
        stub16 = sb5("p5_stub16", [1, 512], BF16)
        mu2bc = sb5("p5_mu2bc", [128, T], BF16)
        rs2bc = sb5("p5_rs2bc", [128, T], BF16)
        with contextlib.ExitStack() as ps6:
            s1_ps = [ps6.enter_context(nc.psum_tensor(f"p5_s1_{i}", [1, 512], F32)) for i in range(4)]
            s2_ps = [ps6.enter_context(nc.psum_tensor(f"p5_s2_{i}", [1, 512], F32)) for i in range(4)]
            ln2_last = 0
            for kc in range(NKC):
                s_sq = pr.op("scalar", lambda e, kc=kc: e.activation(
                    out=sq16b[:], in_=hs16[kc][:], func=AF.Square), sem="SE")
                pr.wait("tensor", "SE", s_sq)
                for tc in range(4):
                    pr.op("tensor", lambda e, kc=kc, tc=tc: e.matmul(
                        out=s1_ps[tc][:], lhsT=ones16[:, 0:1],
                        rhs=hs16[kc][:, tc * 512:(tc + 1) * 512],
                        start=(kc == 0), stop=(kc == NKC - 1)))
                    ln2_last = pr.op("tensor", lambda e, kc=kc, tc=tc: e.matmul(
                        out=s2_ps[tc][:], lhsT=ones16[:, 0:1],
                        rhs=sq16b[:, tc * 512:(tc + 1) * 512],
                        start=(kc == 0), stop=(kc == NKC - 1)),
                        sem="PE" if kc == NKC - 1 else None)
                if kc < NKC - 1:
                    mk = pr.op("tensor", lambda e: e.nop(), sem="PE")
                    pr.wait("scalar", "PE", mk)
            v_dma_prev = 0
            for tc in range(4):
                pr.wait("scalar", "PE", ln2_last)
                pr.wait("scalar", "SQ", v_dma_prev)
                pr.op("scalar", lambda e, tc=tc: e.mul(out=stubA[:], in_=s1_ps[tc][:], mul=1.0 / C))
                pr.op("scalar", lambda e, tc=tc: e.mul(out=stubB[:], in_=s2_ps[tc][:], mul=1.0 / C))
                s_ab = pr.fence("scalar", "SE")
                pr.wait("vector", "SE", s_ab)
                pr.op("vector", lambda e: e.tensor_mul(out=stub16[:], in0=stubA[:], in1=stubA[:]))
                pr.drain("vector")
                pr.op("vector", lambda e: e.tensor_sub(out=stubB[:], in0=stubB[:], in1=stub16[:].bitcast(BF16) if False else stub16[:]))
                v_var = pr.fence("vector", "VE")
                pr.wait("scalar", "VE", v_var)
                pr.op("scalar", lambda e: e.activation(out=stubB[:], in_=stubB[:], func=AF.Ln, bias=CEPSr))
                pr.drain("scalar")
                pr.op("scalar", lambda e: e.activation(out=stubB[:], in_=stubB[:], func=AF.Exp, scale=CNHr))
                pr.drain("scalar")
                pr.op("scalar", lambda e: e.copy(out=stub16[:], in_=stubA[:]))
                s_m = pr.fence("scalar", "SE")
                pr.wait("sync", "SE", s_m)
                pr.op("sync", lambda e, tc=tc: e.dma_start(
                    out=rows16[33:34, tc * 512:(tc + 1) * 512], in_=stub16[:]), sem="SQ", n=16)
                pr.wait("scalar", "SQ", pr.cnt["SQ"])
                pr.op("scalar", lambda e: e.copy(out=stub16[:], in_=stubB[:]))
                s_r = pr.fence("scalar", "SE")
                pr.wait("sync", "SE", s_r)
                v_dma_prev = pr.op("sync", lambda e, tc=tc: e.dma_start(
                    out=rows16[34:35, tc * 512:(tc + 1) * 512], in_=stub16[:]), sem="SQ", n=16)
            pr.wait("sync", "SQ", v_dma_prev)
            pr.op("sync", lambda e: e.dma_start(
                out=mu2bc[:], in_=rows16[33:34, :].partition_broadcast(128)), sem="SQ", n=16)
            v_bc2 = pr.op("sync", lambda e: e.dma_start(
                out=rs2bc[:], in_=rows16[34:35, :].partition_broadcast(128)), sem="SQ", n=16)
            pr.wait("vector", "SQ", v_bc2)
            for kc in range(NKC):
                pr.op("vector", lambda e, kc=kc: e.tensor_sub(
                    out=n2T[kc][:], in0=hs16[kc][:], in1=mu2bc[:]))
                pr.drain("vector")
                pr.op("vector", lambda e, kc=kc: e.tensor_mul(
                    out=n2T[kc][:], in0=n2T[kc][:], in1=rs2bc[:]))
                pr.drain("vector")
                pr.op("vector", lambda e, kc=kc: e.tensor_scalar(
                    out=n2T[kc][:], in0=n2T[kc][:], scalar1=_col(ln2w, kc), scalar2=_col(ln2b, kc),
                    op0=ALU.mult, op1=ALU.add))
            v_n2 = pr.fence("vector", "VE")
            pr.barrier()
        g2_scope = contextlib.ExitStack()
        sbg2 = lambda nm, shape, dt: g2_scope.enter_context(nc.sbuf_tensor(nm, shape, dt, side="right"))
        s_rw2 = gating(ap["sim2"], ap["gates2"], n2T, rwB, sbg2, "g2", v_n2)
        pr.barrier()
        g2_scope.close()

        dbg_dump("n2", n2T, pr, root)
        dbg_dump("rw2rows", [rwB], pr, root)
        pr.barrier()
        p5_ctx.close()
        if upto <= 5:
            _final_wait(pr)
            pr.finalize()
            return nc, dbg_out

        p6_ctx = contextlib.ExitStack()
        sb6 = lambda nm, shape, dt: p6_ctx.enter_context(nc.sbuf_tensor(nm, shape, dt, side="right"))
        rw2bc8 = [sb6(f"p6_rw2bc_{i}", [128, 512], BF16) for i in range(E)]
        w2sb = [sb6(f"p6_w2sb_{i}", [128, 8 * 512], BF16) for i in range(2)]
        geebuf = [sb6(f"p6_ge_{i}", [128, 8 * 512], BF16) for i in range(2)]
        g16b = sb6("p6_g16", [128, 512], BF16)
        F16t = [sb6(f"p6_F_{i}", [128, 512], BF16) for i in range(4)]
        xre = [sb6(f"p6_xre_{i}", [128, 512], F32) for i in range(2)]
        outb = [sb6(f"p6_outb_{i}", [128, 512], F32) for i in range(2)]
        w1cast = lambda par, kc: xnT[par * 4 + kc // 2][:, (kc % 2) * II:(kc % 2 + 1) * II]
        w1sl = lambda par, kc, mi: xnT[par * 4 + kc // 2][:, (kc % 2) * II + mi * 128:(kc % 2) * II + (mi + 1) * 128]
        gesl = lambda par, ki: geebuf[par][:, ki * 512:(ki + 1) * 512]
        w2sl = lambda par, ki: w2sb[par][:, ki * 512:(ki + 1) * 512]
        with contextlib.ExitStack() as ps7:
            mo_ps = ps7.enter_context(nc.psum_tensor("p6_mo", [128, 2048], F32))
            in_ps = [ps7.enter_context(nc.psum_tensor(f"p6_in_{i}", [128, 512], F32)) for i in range(2)]
            rb_ps = ps7.enter_context(nc.psum_tensor("p6_rb", [128, 512], F32))
            ftr_ps = ps7.enter_context(nc.psum_tensor("p6_ftr", [128, 128], BF16))
            w1_pe2 = [0, 0]
            w1_ld = [0, 0]
            w2_ld = [0, 0]
            w2_pe2 = [0, 0]
            ge_pe2 = [0, 0]
            ge_dma2 = [0, 0]
            inps_ev = [0, 0]
            silu_ev = 0
            f_pe = 0
            rbq_used = 0
            out_dma = [0, 0]
            xre_used = [0, 0]
            v_ge2 = [0, 0]

            def w1_load(tc, ee):
                par = ee % 2
                pr.wait("sync", "PE", w1_pe2[par])
                last = 0
                for tl in range(4):
                    last = pr.op("sync", lambda e, ee=ee, tl=tl, par=par: e.dma_start(
                        out=xnT[par * 4 + tl][:].rearrange("p (two i) -> p two i", i=II),
                        in_=w1b_scr[ee][tl * 256:(tl + 1) * 256, :].rearrange("(two p) i -> p two i", p=128)), sem="SQ", n=16)
                w1_ld[par] = last

            def w2_load(ph, ee):
                par = ee % 2
                pr.wait("scalar", "PE", w2_pe2[par])
                w2_ld[par] = pr.op("scalar", lambda e, ee=ee, ph=ph, par=par: e.dma_start(
                    out=w2sb[par][:].rearrange("p (ki c) -> p ki c", c=512),
                    in_=w2b_scr[ee][:, ph * 512:(ph + 1) * 512].rearrange("(ki p) c -> p ki c", p=128)), sem="AQ", n=16)

            def ge_load(tc, ee):
                par = ee % 2
                pr.wait("scalar", "PE", ge_pe2[par])
                ge_dma2[par] = pr.op("scalar", lambda e, tc=tc, ee=ee, par=par: e.dma_start(
                    out=geebuf[par][:], in_=ge_scr[tc, ee]), sem="AQ", n=16)

            for ph in range(2):
                for tc in range(4):
                    pr.wait("tensor", "SE", silu_ev)
                    for mcr in range(4):
                        pr.op("tensor", lambda e, ph=ph, tc=tc, mcr=mcr: e.matmul(
                            out=mo_ps[:, mcr * 512:(mcr + 1) * 512], lhsT=ident16[:],
                            rhs=hs16[ph * 4 + mcr][:, tc * 512:(tc + 1) * 512],
                            start=True, stop=False))
                    w2_last = 0
                    if ph == 0:
                        w1_load(tc, 0)
                        w2_load(ph, 0)
                        pr.wait("tensor", "VE", rbq_used)
                        s_rb = 0
                        for ee in range(E):
                            pr.wait("tensor", "SE", s_rb)
                            pe_rb = pr.op("tensor", lambda e, ee=ee, tc=tc: e.matmul(
                                out=rb_ps[:, (ee % 2) * 256:(ee % 2) * 256 + 256][:, 0:256] if False else rb_ps[:],
                                lhsT=sel8[:, ee * 128:(ee + 1) * 128],
                                rhs=rwB[0:8, tc * 512:(tc + 1) * 512], start=True, stop=True), sem="PE")
                            pr.wait("scalar", "PE", pe_rb)
                            s_rb = pr.op("scalar", lambda e, ee=ee: e.copy(out=rw2bc8[ee][:], in_=rb_ps[:]), sem="SE")
                        s_rb_all = s_rb
                    else:
                        ge_load(tc, 0)
                        w2_load(ph, 0)
                    for ee in range(E):
                        par = ee % 2
                        if ph == 0:
                            if ee + 1 < E:
                                w1_load(tc, ee + 1)
                                w2_load(ph, ee + 1)
                            pr.wait("tensor", "SQ", w1_ld[par])
                            pr.wait("tensor", "AQ", ge_dma2[par])
                            pr.wait("tensor", "PE", ge_pe2[par])
                            w1_last = 0
                            for mi in range(NKC):
                                pb = mi % 2
                                pr.wait("tensor", "SE", inps_ev[pb])
                                for kc in range(NKC):
                                    w1_last = pr.op("tensor", lambda e, mi=mi, kc=kc, tc=tc, pb=pb, par=par: e.matmul(
                                        out=in_ps[pb][:], lhsT=w1sl(par, kc, mi),
                                        rhs=n2T[kc][:, tc * 512:(tc + 1) * 512],
                                        start=(kc == 0), stop=(kc == NKC - 1)),
                                        sem="PE" if kc == NKC - 1 else None)
                                pr.wait("scalar", "PE", w1_last)
                                s_ge = pr.op("scalar", lambda e, mi=mi, pb=pb, par=par: e.activation(
                                    out=gesl(par, mi), in_=in_ps[pb][:], func=AF.Gelu), sem="SE")
                                inps_ev[pb] = s_ge
                                pr.wait("vector", "SE", max(s_ge, s_rb_all))
                                pr.op("vector", lambda e, mi=mi, par=par, ee=ee: e.tensor_mul(
                                    out=gesl(par, mi), in0=gesl(par, mi), in1=rw2bc8[ee][:]))
                            w1_pe2[par] = w1_last
                            v_ge2[par] = pr.fence("vector", "VE")
                            rbq_used = v_ge2[par]
                            pr.wait("scalar", "VE", v_ge2[par])
                            ge_dma2[par] = pr.op("scalar", lambda e, tc=tc, ee=ee, par=par: e.dma_start(
                                out=ge_scr[tc, ee], in_=geebuf[par][:]), sem="AQ", n=16)
                        else:
                            if ee + 1 < E:
                                ge_load(tc, ee + 1)
                                w2_load(ph, ee + 1)
                        pr.wait("tensor", "AQ", w2_ld[par])
                        if ph == 0:
                            pr.wait("tensor", "VE", v_ge2[par])
                        else:
                            pr.wait("tensor", "AQ", ge_dma2[par])
                        for mcr in range(4):
                            for ki in range(NKC):
                                w2_last = pr.op("tensor", lambda e, mcr=mcr, ki=ki, ee=ee, par=par: e.matmul(
                                    out=mo_ps[:, mcr * 512 + 0:mcr * 512 + 512],
                                    lhsT=w2sl(par, ki)[:, mcr * 128:(mcr + 1) * 128],
                                    rhs=gesl(par, ki),
                                    start=False, stop=(ee == E - 1 and ki == NKC - 1)),
                                    sem="PE" if (ki == NKC - 1 and mcr == 3) else None)
                        ge_pe2[par] = w2_last
                        w2_pe2[par] = w2_last
                    pr.wait("scalar", "PE", w2_last)
                    v_F = 0
                    for mcr in range(4):
                        mc = ph * 4 + mcr
                        pr.wait("scalar", "PE", f_pe)
                        pr.wait("scalar", "VE", v_F)
                        s_g = pr.op("scalar", lambda e, mcr=mcr: e.activation(
                            out=g16b[:], in_=mo_ps[:, mcr * 512:(mcr + 1) * 512], func=AF.Silu), sem="SE")
                        silu_ev = s_g
                        pr.wait("vector", "SE", s_g)
                        pr.op("vector", lambda e, mcr=mcr, mc=mc, tc=tc: e.tensor_mul(
                            out=F16t[mcr][:], in0=g16b[:], in1=sout16[mc][:, tc * 512:(tc + 1) * 512]))
                        v_F = pr.fence("vector", "VE")
                    for tb in range(4):
                        tt = tc * 4 + tb
                        xb2 = tb % 2
                        pr.wait("sync", "VE", xre_used[xb2])
                        v_x = pr.op("sync", lambda e, tt=tt, ph=ph, xb2=xb2: e.dma_start(
                            out=xre[xb2][:], in_=ap["x"][tt * 128:(tt + 1) * 128, ph * 512:(ph + 1) * 512]), sem="SQ", n=16)
                        pr.wait("vector", "SQ", max(v_x, out_dma[xb2]))
                        for mcr in range(4):
                            pr.wait("tensor", "VE", v_F)
                            pe_f = pr.op("tensor", lambda e, mcr=mcr, tb=tb: e.transpose(
                                out=ftr_ps[:], in_=F16t[mcr][:, tb * 128:(tb + 1) * 128], identity=ident16[:]), sem="PE")
                            pr.wait("vector", "PE", pe_f)
                            v_o = pr.op("vector", lambda e, mcr=mcr, xb2=xb2: e.tensor_add(
                                out=outb[xb2][:, mcr * 128:(mcr + 1) * 128], in0=ftr_ps[:],
                                in1=xre[xb2][:, mcr * 128:(mcr + 1) * 128]), sem="VE")
                            pr.wait("tensor", "VE", v_o)
                        xre_used[xb2] = v_o
                        f_pe = pr.cnt["PE"]
                        pr.wait("sync", "VE", v_o)
                        out_dma[xb2] = pr.op("sync", lambda e, tt=tt, ph=ph, xb2=xb2: e.dma_start(
                            out=ap["out"][tt * 128:(tt + 1) * 128, ph * 512:(ph + 1) * 512],
                            in_=outb[xb2][:]), sem="SQ", n=16)
                pr.barrier()
        p6_ctx.close()

        _final_wait(pr)
        pr.finalize()
        return nc, dbg_out


def _final_wait(pr):
    pr.wait("sync", "SQ", pr.cnt["SQ"])
    pr.wait("sync", "AQ", pr.cnt["AQ"])


def _host_consts():
    ident = np.eye(128, dtype=np.float32)
    ones = np.ones((128, 128), dtype=np.float32)
    sel8 = np.zeros((E, E * 128), dtype=np.float32)
    for e in range(E):
        sel8[e, e * 128:(e + 1) * 128] = 1.0
    dmask = np.where(np.arange(128)[None, :] <= np.arange(128)[:, None],
                     0.0, NEGBIG).astype(np.float32)
    return {
        "ident16": ident.astype(ml_dtypes.bfloat16),
        "ident32": ident,
        "ones16": ones.astype(ml_dtypes.bfloat16),
        "ones32": ones,
        "sel8": sel8.astype(ml_dtypes.bfloat16),
        "dmask": dmask,
    }


def _in_map(core_x, inputs):
    m = {
        "x": np.ascontiguousarray(core_x),
        "w_in": inputs["L0_attn_W_in"],
        "conv_w": np.ascontiguousarray(np.asarray(inputs["L0_attn_conv_w"])[:, 0, :]),
        "conv_b": np.asarray(inputs["L0_attn_conv_b"])[None, :],
        "a_log": inputs["L0_attn_A_log"],
        "w_x": inputs["L0_attn_W_x"],
        "w_out": inputs["L0_attn_W_out"],
        "ln1_w": np.asarray(inputs["L0_ffn_ln1_w"])[None, :],
        "ln1_b": np.asarray(inputs["L0_ffn_ln1_b"])[None, :],
        "ln2_w": np.asarray(inputs["L0_ffn_ln2_w"])[None, :],
        "ln2_b": np.asarray(inputs["L0_ffn_ln2_b"])[None, :],
        "sim1": inputs["L0_ffn_smha_sim"],
        "gates1": np.asarray(inputs["L0_ffn_smha_gates"])[None, :],
        "qp": inputs["L0_ffn_q_proj"],
        "kp": inputs["L0_ffn_k_proj"],
        "vp": inputs["L0_ffn_v_proj"],
        "op": inputs["L0_ffn_o_proj"],
        "sim2": inputs["L0_ffn_moe_sim"],
        "gates2": np.asarray(inputs["L0_ffn_moe_gates"])[None, :],
        "w1": inputs["L0_ffn_w1"],
        "w2": inputs["L0_ffn_w2"],
    }
    m = {k: np.ascontiguousarray(np.asarray(v, dtype=np.float32)) for k, v in m.items()}
    m.update(_host_consts())
    return m


def kernel(**inputs):
    x = np.asarray(inputs["L0_attn_x"], dtype=np.float32)
    nc, dbg_out = build(upto=UPTO, dbg=DBG)
    in_maps = [_in_map(x[b], inputs) for b in range(8)]
    res = run_bass_kernel_spmd(nc, in_maps, core_ids=list(range(8)))
    out = np.stack([np.asarray(res.results[b]["out"], dtype=np.float32) for b in range(8)], axis=0)
    if DBG:
        kernel._dbg = res
    return out
